# version 03
# baseline (speedup 1.0000x reference)
import numpy as np
import ml_dtypes
from contextlib import ExitStack

import concourse.bass as bass
import concourse.mybir as mybir
from concourse.bass_utils import run_bass_kernel_spmd

F32 = mybir.dt.float32
BF16 = mybir.dt.bfloat16
AF = mybir.ActivationFunctionType
ALU = mybir.AluOpType
AXX = mybir.AxisListType.X

D = 2048
NCH = 16
OWN = 2048
HALO = 512
EXT = OWN + HALO
FULL = 4096
X9T = HALO + FULL
FF = 5632
HQ, HKV, HD = 16, 4, 128
MQ = 64
MTOT = HQ * MQ
NEG = -1.0e30
EPS_LN = 1.0e-5
EPS_RMS = 1.0e-6
EPS_LA = 1.0e-6

ENGINES = ("sync", "vector", "scalar", "tensor", "gpsimd")


class _Sem:
    __slots__ = ("h", "n", "name")


class Prog:

    def __init__(self, nc, ctx):
        self.nc = nc
        self.ctx = ctx
        self.ops = {e: [] for e in ENGINES}
        self.engine_sems = {}
        self.PH = self.sem("PH")

    def sem(self, name):
        s = _Sem()
        s.h = self.ctx.enter_context(self.nc.semaphore(name))
        s.n = 0
        s.name = name
        return s

    def do(self, eng, fn=None, wait=(), inc=(), raw=False):
        es = self.engine_sems.get(eng)
        if es is not None and fn is not None and not raw:
            wait = list(wait) + [(es, es.n)]
            inc = list(inc) + [(es, 1)]
            merged = {}
            for (s, k) in inc:
                merged[id(s)] = (s, merged.get(id(s), (s, 0))[1] + k)
            inc = list(merged.values())
        w = [(s.h, s.n if v is None else v) for (s, v) in wait]
        for (s, k) in inc:
            s.n += k
        i = [(s.h, k) for (s, k) in inc]
        self.ops[eng].append((fn, w, i))

    def barrier(self, dma_sems=()):
        for s in dma_sems:
            q = "gpsimd" if s.name == "ST" else "sync"
            self.do(q, None, wait=[(s, s.n)])
        for e in ENGINES:
            self.do(e, lambda eng: eng.nop(), inc=[(self.PH, 1)], raw=True)
        tgt = self.PH.n
        for e in ENGINES:
            self.do(e, None, wait=[(self.PH, tgt)])

    def replay(self, block):
        def make(en):
            def body(eng):
                for (fn, waits, incs) in self.ops[en]:
                    for (h, v) in waits:
                        eng.wait_ge(h, v)
                    if fn is None:
                        continue
                    instr = fn(eng)
                    for (h, k) in incs:
                        instr.then_inc(h, k)
            return body
        for en in ENGINES:
            getattr(block, en)(make(en))


def fm(dram2d):
    return dram2d.rearrange("(c p) t -> p c t", p=128)


class Builder:
    def __init__(self):
        self.nc = nc = bass.Bass("TRN2", target_bir_lowering=False, debug=False,
                                 detect_race_conditions=True)
        self.ctx = ExitStack()
        self.P = Prog(nc, self.ctx)
        self.LD = self.P.sem("LD")
        self.ST = self.P.sem("ST")
        self.Lp = [self.P.sem(f"Lp{i}") for i in range(4)]
        self.MM = self.P.sem("MM")
        self.EV = self.P.sem("EV")
        self.EA = self.P.sem("EA")
        self.P.engine_sems = {"vector": self.EV, "scalar": self.EA}

        def din(name, shape, dt=F32):
            return nc.dram_tensor(name, shape, dt, kind="ExternalInput").ap()
        self.inp = {}
        for name, shape, dt in [
            ("x9", [D, X9T], F32),
            ("wq_a", [D, D], F32), ("wk_a", [D, D], F32),
            ("wvTa", [D, D], F32), ("woTa", [D, D], F32),
            ("wmq", [MQ, HD], F32), ("wmk", [MQ, HD], F32),
            ("bq_a", [D], F32), ("bk_a", [D], F32),
            ("bv_a", [D], F32), ("bo_a", [D], F32),
            ("Wq_b", [D, D], F32), ("Wk_b", [D, HKV * HD], F32),
            ("Wv_b", [D, HKV * HD], F32), ("Wo_b", [D, D], F32),
            ("wvTf", [D, D], F32), ("wo_f", [D, D], F32),
            ("w_r", [D], F32), ("b_r", [1], F32),
            ("n1w", [D], F32), ("n2w", [D], F32), ("bv_f", [D], F32),
            ("Wg", [D, FF], F32), ("Wu", [D, FF], F32), ("Wd", [FF, D], F32),
            ("maskR", [128, 640], BF16), ("maskF", [4 * 128, 640], BF16),
            ("IDN", [128, 128], BF16), ("IDNF", [128, 128], F32),
            ("onesb", [128, 4], BF16), ("ones1f", [1, 128], F32),
            ("epsv", [1, 2], F32),
        ]:
            self.inp[name] = din(name, shape, dt)

        self.out = nc.dram_tensor("out", [D, OWN], F32,
                                  kind="ExternalOutput").ap()

        dscr = lambda name, shape, dt: nc.dram_tensor(name, shape, dt).ap()
        self.xr = dscr("xr", [D, EXT], F32)
        self.lnbuf = dscr("lnbuf", [D, EXT], BF16)
        self.wqmD = dscr("wqmD", [D, MTOT], BF16)
        self.wkmD = dscr("wkmD", [D, MTOT], BF16)
        self.W2D = dscr("W2D", [D, D], BF16)
        self.WgD = dscr("WgD", [D, FF], BF16)
        self.WuD = dscr("WuD", [D, FF], BF16)
        self.WdD = dscr("WdD", [FF, D], BF16)
        self.woTaD = dscr("woTaD", [D, D], BF16)
        self.WobD = dscr("WobD", [D, D], BF16)
        self.bnc = dscr("bnc", [4096], F32)

        self.taps = {"WobD": (self.WobD, BF16, [D, D]),
                     "xr": (self.xr, F32, [D, EXT]),
                     "lnbuf": (self.lnbuf, BF16, [D, EXT]),
                     "wqmD": (self.wqmD, BF16, [D, MTOT]),
                     "wkmD": (self.wkmD, BF16, [D, MTOT]),
                     "W2D": (self.W2D, BF16, [D, D]),
                     "WgD": (self.WgD, BF16, [D, FF]),
                     "WdD": (self.WdD, BF16, [FF, D]),
                     "woTaD": (self.woTaD, BF16, [D, D])}

        self._names = {}
        self.sbt = lambda name, shape, dt: self.ctx.enter_context(
            nc.sbuf_tensor(name, shape, dt))
        sbt = self.sbt
        self.sIDN = sbt("sIDN", [128, 128], BF16)
        self.sIDNF = sbt("sIDNF", [128, 128], F32)
        self.sOnes = sbt("sOnes", [128, 4], BF16)
        self.sOnes1f = sbt("sOnes1f", [1, 128], F32)
        self.sWrC = sbt("sWrC", [128, NCH], F32)
        self.sBoC = sbt("sBoC", [128, NCH], F32)
        self.sBW2C = sbt("sBW2C", [128, NCH], F32)
        self.sN1C = sbt("sN1C", [128, NCH], F32)
        self.sN2C = sbt("sN2C", [128, NCH], F32)
        self.sBqmC = sbt("sBqmC", [128, 8], F32)
        self.sKsumC = sbt("sKsumC", [128, 8], F32)
        self.sRow = sbt("sRow", [1, 5632], F32)
        self.sTiny = sbt("sTiny", [1, 8], F32)
        self.kv_ctx = ExitStack()

    def SB(self, name, shape, dt):
        k = self._names.get(name, 0)
        self._names[name] = k + 1
        return self.nc.sbuf_tensor(name if k == 0 else f"{name}_{k}",
                                   shape, dt)

    def PS(self, name, shape, dt=F32):
        k = self._names.get(name, 0)
        self._names[name] = k + 1
        return self.nc.psum_tensor(name if k == 0 else f"{name}_{k}",
                                   shape, dt)

    def bar(self):
        self.P.barrier(dma_sems=[self.LD, self.ST] + self.Lp)

    def load(self, dst_ap, src_ap, s=None):
        s = s or self.LD
        self.P.do("sync", None, wait=[(s, s.n)])
        self.P.do("sync", lambda e, d=dst_ap, r=src_ap: e.dma_start(out=d, in_=r),
                  inc=[(s, 16)])
        return (s, s.n)

    def store(self, dst_ap, src_ap, wait=()):
        s = self.ST
        self.P.do("gpsimd", None, wait=[(s, s.n)] + list(wait))
        self.P.do("gpsimd", lambda e, d=dst_ap, r=src_ap: e.dma_start(
            out=d, in_=r), inc=[(s, 16)])
        return (s, s.n)

    def mmv(self, out_ap, lhsT_ap, rhs_ap, start, stop, wait=(), inc=()):
        assert lhsT_ap.base_partition() == rhs_ap.base_partition(), (
            lhsT_ap, rhs_ap)
        self.P.do("tensor", lambda e, o=out_ap, l=lhsT_ap, r=rhs_ap, s=start,
                  t=stop: e.matmul(o, l, r, start=s, stop=t), wait=wait, inc=inc)

    def vec(self, fn, wait=(), inc=()):
        self.P.do("vector", fn, wait=wait, inc=inc)

    def act(self, fn, wait=(), inc=()):
        self.P.do("scalar", fn, wait=wait, inc=inc)

    def colT(self, vec_dram, dst_col, n, t16, pT, extra_wait=()):
        P, LD, MM, EV = self.P, self.LD, self.MM, self.EV
        m = self.load(t16[0:n, 0:128], vec_dram.rearrange("(c p) -> c p", p=128))
        P.do("tensor", lambda e: e.transpose(pT[:, 0:n], t16[0:n, 0:128],
                                             self.sIDNF[0:n, 0:n]),
             wait=[m, (self.EV, self.EV.n)] + list(extra_wait),
             inc=[(MM, 1)])
        self.vec(lambda e: e.tensor_copy(dst_col[:, 0:n], pT[:, 0:n]),
                 wait=[(MM, None)], inc=[(EV, 1)])

    def p0a(self):
        P, LD, ST, MM, EV, EA = self.P, self.LD, self.ST, self.MM, self.EV, self.EA
        nc, I = self.nc, self.inp
        self.load(self.sIDN[:], I["IDN"])
        self.load(self.sIDNF[:], I["IDNF"])
        self.load(self.sOnes[:], I["onesb"])
        self.load(self.sOnes1f[:], I["ones1f"])
        self.load(self.sTiny[0:1, 1:2], I["b_r"][None, :])
        self.load(self.sTiny[0:1, 2:4], I["epsv"])
        m = self.load(self.sRow[0:1, 0:D], I["w_r"][None, :])
        self.vec(lambda e: e.reduce_sum(self.sTiny[0:1, 0:1],
                                        self.sRow[0:1, 0:D], axis=AXX),
                 wait=[m], inc=[(EV, 1)])

        _sc242 = ExitStack()
        t16 = _sc242.enter_context(self.SB('t16', [NCH, 128], F32))
        pT0 = _sc242.enter_context(self.PS('pT0', [128, 64], F32))
        mI = (self.LD, self.LD.n)
        for vec, dst in ((I["w_r"], self.sWrC), (I["bo_a"], self.sBoC),
                         (I["n1w"], self.sN1C), (I["n2w"], self.sN2C)):
            self.colT(vec, dst, NCH, t16, pT0, extra_wait=[mI])
            P.do("sync", None, wait=[(EV, None)])
        _sc242.close()
        self.bar()

        _sc254 = ExitStack()
        wmF = _sc254.enter_context(self.SB('wmF', [MQ, 2, HD], F32))
        wmFb = _sc254.enter_context(self.SB('wmFb', [MQ, 2, HD], BF16))
        wmT = _sc254.enter_context(self.SB('wmT', [128, 2, MQ], BF16))
        bqk = _sc254.enter_context(self.SB('bqk', [NCH, 2, 128], F32))
        bqkb = _sc254.enter_context(self.SB('bqkb', [NCH, 2, 128], BF16))
        bqkT = _sc254.enter_context(self.SB('bqkT', [128, 2 * NCH], BF16))
        wrowq = _sc254.enter_context(self.SB('wrowq', [128, 2, D], F32))
        wrowk = _sc254.enter_context(self.SB('wrowk', [128, 2, D], F32))
        wrowqb = _sc254.enter_context(self.SB('wrowqb', [128, 2, D], BF16))
        wrowkb = _sc254.enter_context(self.SB('wrowkb', [128, 2, D], BF16))
        wqmS = _sc254.enter_context(self.SB('wqmS', [128, NCH, MTOT], BF16))
        wkmS = _sc254.enter_context(self.SB('wkmS', [128, NCH, MTOT], BF16))
        bqmS = _sc254.enter_context(self.SB('bqmS', [NCH, 2 * MQ], F32))
        tb8 = _sc254.enter_context(self.SB('tb8', [8, 128], F32))
        pQm = _sc254.enter_context(self.PS('pQm', [128, MTOT], F32))
        pKm = _sc254.enter_context(self.PS('pKm', [128, MTOT], F32))
        pBm = _sc254.enter_context(self.PS('pBm', [NCH, 2 * MQ], F32))
        pMt = _sc254.enter_context(self.PS('pMt', [128, 2 * MQ + 2 * NCH], BF16))
        pT1 = _sc254.enter_context(self.PS('pT1', [128, 8], F32))
        self.load(wmF[:, 0, :], I["wmq"])
        self.load(wmF[:, 1, :], I["wmk"])
        self.load(bqk[:, 0, :], I["bq_a"].rearrange("(h d) -> h d", d=128))
        m = self.load(bqk[:, 1, :], I["bk_a"].rearrange("(h d) -> h d", d=128))
        self.vec(lambda e: e.tensor_copy(wmFb[:], wmF[:]), wait=[m])
        self.vec(lambda e: e.tensor_copy(bqkb[:], bqk[:]), inc=[(EV, 1)])
        ev0 = EV.n
        P.do("tensor", lambda e: e.transpose(pMt[:, 0:MQ], wmFb[:, 0, :],
                                             self.sIDN[0:MQ, 0:MQ]),
             wait=[(EV, ev0)])
        P.do("tensor", lambda e: e.transpose(pMt[:, MQ:2 * MQ],
                                             wmFb[:, 1, :],
                                             self.sIDN[0:MQ, 0:MQ]))
        P.do("tensor", lambda e: e.transpose(
            pMt[:, 2 * MQ:2 * MQ + NCH], bqkb[:, 0, :],
            self.sIDN[0:NCH, 0:NCH]))
        P.do("tensor", lambda e: e.transpose(
            pMt[:, 2 * MQ + NCH:2 * MQ + 2 * NCH], bqkb[:, 1, :],
            self.sIDN[0:NCH, 0:NCH]), inc=[(MM, 1)])
        mmT = MM.n
        self.vec(lambda e: e.tensor_copy(
            wmT[:].rearrange("p a m -> p (a m)"), pMt[:, 0:2 * MQ]),
            wait=[(MM, mmT)])
        self.vec(lambda e: e.tensor_copy(bqkT[:],
                                         pMt[:, 2 * MQ:2 * MQ + 2 * NCH]),
                 inc=[(EV, 1)])
        evT = EV.n
        self.mmv(pBm[:, 0:MQ], bqkT[:, 0:NCH], wmT[:, 0, :], True, True,
                 wait=[(EV, evT)])
        self.mmv(pBm[:, MQ:2 * MQ], bqkT[:, NCH:2 * NCH], wmT[:, 1, :],
                 True, True, inc=[(MM, 1)])
        self.vec(lambda e: e.tensor_copy(bqmS[:], pBm[:]),
                 wait=[(MM, None)], inc=[(EV, 1)])
        P.do("gpsimd", None, wait=[(ST, ST.n)])
        P.do("gpsimd", lambda e: e.dma_start(
            out=self.bnc[0:MTOT].rearrange("(h m) -> h m", m=MQ),
            in_=bqmS[:, 0:MQ]), wait=[(EV, None)], inc=[(ST, 16)])
        P.do("gpsimd", None, wait=[(ST, ST.n)])
        P.do("gpsimd", lambda e: e.dma_start(
            out=self.bnc[MTOT:2 * MTOT].rearrange("(h m) -> h m", m=MQ),
            in_=bqmS[:, MQ:2 * MQ]), inc=[(ST, 16)])
        P.do("sync", None, wait=[(ST, None)])
        m = self.load(tb8[:], self.bnc[0:MTOT].rearrange("(c p) -> c p",
                                                         p=128))
        P.do("tensor", lambda e: e.transpose(pT1[:], tb8[:],
                                             self.sIDNF[0:8, 0:8]),
             wait=[m], inc=[(MM, 1)])
        self.vec(lambda e: e.tensor_copy(self.sBqmC[:], pT1[:]),
                 wait=[(MM, None)], inc=[(EV, 1)])
        ev_cast = {}
        mm_done = {}
        for hh in range(HQ):
            b = hh % 2
            if hh >= 2:
                P.do("sync", None, wait=[(EV, ev_cast[hh - 2])])
            self.load(wrowq[:, b, :], I["wq_a"][hh * 128:(hh + 1) * 128, :],
                      s=self.Lp[b])
            mload = self.load(wrowk[:, b, :],
                              I["wk_a"][hh * 128:(hh + 1) * 128, :],
                              s=self.Lp[b])
            w = [mload]
            if hh >= 2:
                w.append((MM, mm_done[hh - 2]))
            self.vec(lambda e, b=b: e.tensor_copy(wrowqb[:, b, :],
                                                  wrowq[:, b, :]), wait=w)
            self.vec(lambda e, b=b: e.tensor_copy(wrowkb[:, b, :],
                                                  wrowk[:, b, :]),
                     inc=[(EV, 1)])
            ev_cast[hh] = EV.n
            w0 = [(EV, ev_cast[hh])]
            if hh >= 1:
                w0.append((EV, ev_cast.get(("evac", hh - 1), 0)))
            for dc in range(NCH):
                self.mmv(pQm[:, dc * MQ:(dc + 1) * MQ],
                         wrowqb[:, b, dc * 128:(dc + 1) * 128],
                         wmT[:, 0, :], True, True,
                         wait=w0 if dc == 0 else ())
                self.mmv(pKm[:, dc * MQ:(dc + 1) * MQ],
                         wrowkb[:, b, dc * 128:(dc + 1) * 128],
                         wmT[:, 1, :], True, True,
                         inc=[(MM, 1)] if dc == NCH - 1 else ())
            mm_done[hh] = MM.n
            self.vec(lambda e, hh=hh: e.tensor_copy(
                wqmS[:, :, hh * MQ:(hh + 1) * MQ],
                pQm[:].rearrange("p (c m) -> p c m", m=MQ)),
                wait=[(MM, mm_done[hh])])
            self.vec(lambda e, hh=hh: e.tensor_copy(
                wkmS[:, :, hh * MQ:(hh + 1) * MQ],
                pKm[:].rearrange("p (c m) -> p c m", m=MQ)),
                inc=[(EV, 1)])
            ev_cast[("evac", hh)] = EV.n
        P.do("gpsimd", None, wait=[(ST, ST.n)])
        P.do("gpsimd", lambda e: e.dma_start(out=fm(self.wqmD), in_=wqmS[:]),
             wait=[(EV, None)], inc=[(ST, 16)])
        P.do("gpsimd", None, wait=[(ST, ST.n)])
        P.do("gpsimd", lambda e: e.dma_start(out=fm(self.wkmD), in_=wkmS[:]),
             inc=[(ST, 16)])
        P.do("gpsimd", None, wait=[(ST, None)])
        _sc254.close()
        self.bar()

    def p0b(self):
        P, LD, ST, MM, EV, EA = self.P, self.LD, self.ST, self.MM, self.EV, self.EA
        nc, I = self.nc, self.inp
        _sc378 = ExitStack()
        wvTb = _sc378.enter_context(self.SB('wvTb', [128, NCH, D], BF16))
        wob = _sc378.enter_context(self.SB('wob', [128, NCH, D], BF16))
        wldA = _sc378.enter_context(self.SB('wldA', [128, 2, D], F32))
        bvT = _sc378.enter_context(self.SB('bvT', [128, NCH], F32))
        bvTb = _sc378.enter_context(self.SB('bvTb', [128, NCH], BF16))
        t16 = _sc378.enter_context(self.SB('t16', [NCH, 128], F32))
        w2st = _sc378.enter_context(self.SB('w2st', [128, 2, D], BF16))
        pW = _sc378.enter_context(self.PS('pW', [128, 512], F32))
        pW2 = _sc378.enter_context(self.PS('pW2', [128, 512], F32))
        pT0 = _sc378.enter_context(self.PS('pT0', [128, 64], F32))
        pRw = _sc378.enter_context(self.PS('pRw', [1, 2048], F32))
        self.colT(I["bv_f"], bvT, NCH, t16, pT0)
        self.vec(lambda e: e.tensor_copy(bvTb[:], bvT[:]), wait=[(EV, None)],
                 inc=[(EV, 1)])
        ev_cast = {}
        for i in range(2 * NCH):
            mat, dstb = (I["wvTf"], wvTb) if i % 2 == 0 else (I["wo_f"], wob)
            cc = i // 2
            b = i % 2
            if i >= 2:
                P.do("sync", None, wait=[(EV, ev_cast[i - 2])])
            m = self.load(wldA[:, b, :], mat[cc * 128:(cc + 1) * 128, :],
                          s=self.Lp[b])
            self.vec(lambda e, b=b, dstb=dstb, cc=cc: e.tensor_copy(
                dstb[:, cc, :], wldA[:, b, :]), wait=[m],
                inc=[(EV, 1)])
            ev_cast[i] = EV.n
        evAll = EV.n
        ev_stage = {}
        for dc in range(NCH):
            b = dc % 2
            if dc >= 2:
                pass
        mmN = 0
        stw = {}
        for dc in range(NCH):
            b = dc % 2
            for eg in range(4):
                pp = pW if eg % 2 == 0 else pW2
                for cc in range(NCH):
                    w = [(EV, evAll)] if (dc == 0 and eg == 0 and cc == 0) else ()
                    if cc == 0 and (dc * 4 + eg) >= 2:
                        w = [(EV, ev_stage[dc * 4 + eg - 2])]
                    self.mmv(pp[:, 0:512],
                             wvTb[:, cc, dc * 128:(dc + 1) * 128],
                             wob[:, cc, eg * 512:(eg + 1) * 512],
                             cc == 0, cc == NCH - 1, wait=w,
                             inc=[(MM, 1)] if cc == NCH - 1 else ())
                mmN = self.MM.n
                wv2 = [(MM, mmN)]
                if eg == 0 and dc >= 2:
                    wv2.append((ST, stw[dc - 2]))
                self.vec(lambda e, b=b, dc=dc, eg=eg, pp=pp: e.tensor_scalar(
                    out=w2st[:, b, eg * 512:(eg + 1) * 512], in0=pp[:, 0:512],
                    scalar1=self.sN1C[:, dc:dc + 1], scalar2=None,
                    op0=ALU.mult), wait=wv2, inc=[(EV, 1)])
                ev_stage[dc * 4 + eg] = EV.n
            P.do("gpsimd", None, wait=[(ST, ST.n)])
            P.do("gpsimd", lambda e, b=b, dc=dc: e.dma_start(
                out=fm(self.W2D)[:, dc, :], in_=w2st[:, b, :]),
                wait=[(EV, ev_stage[dc * 4 + 3])], inc=[(ST, 16)])
            stw[dc] = ST.n
        for eg in range(4):
            for cc in range(NCH):
                self.mmv(pRw[0:1, eg * 512:(eg + 1) * 512],
                         bvTb[:, cc:cc + 1], wob[:, cc, eg * 512:(eg + 1) * 512],
                         cc == 0, cc == NCH - 1,
                         inc=[(MM, 1)] if cc == NCH - 1 else ())
        self.vec(lambda e: e.tensor_copy(self.sRow[0:1, 0:2048], pRw[:]),
                 wait=[(MM, None)], inc=[(EV, 1)])
        P.do("gpsimd", None, wait=[(ST, ST.n)])
        P.do("gpsimd", None, wait=[(ST, ST.n)])
        P.do("gpsimd", lambda e: e.dma_start(
            out=self.bnc[2048:2048 + D][None, :],
            in_=self.sRow[0:1, 0:2048]),
             wait=[(EV, None)], inc=[(ST, 16)])
        P.do("sync", None, wait=[(ST, None)])
        self.colT(self.bnc[2048:2048 + D], self.sBW2C, NCH, t16, pT0)
        _sc378.close()
        self.bar()

    def p0c(self):
        P, LD, ST, MM, EV = self.P, self.LD, self.ST, self.MM, self.EV
        nc, I = self.nc, self.inp
        jobs = [(I["Wg"], self.WgD, NCH, FF, True),
                (I["Wu"], self.WuD, NCH, FF, True),
                (I["Wd"], self.WdD, FF // 128, D, False),
                (I["woTa"], self.woTaD, NCH, D, False),
                (I["Wo_b"], self.WobD, NCH, D, False)]
        _sc464 = ExitStack()
        cw = _sc464.enter_context(self.SB('cw', [128, 2, FF], F32))
        cwb = _sc464.enter_context(self.SB('cwb', [128, 2, FF], BF16))
        ev_done = {}
        st_done = {}
        i = 0
        for mat, dst, nrow, ncol, scale in jobs:
            for rr in range(nrow):
                b = i % 2
                if i >= 2:
                    P.do("sync", None, wait=[(ST, st_done[i - 2])])
                m = self.load(cw[:, b, 0:ncol],
                              mat[rr * 128:(rr + 1) * 128, :],
                              s=self.Lp[b])
                if scale:
                    self.vec(lambda e, b=b, rr=rr, ncol=ncol: e.tensor_scalar(
                        out=cwb[:, b, 0:ncol], in0=cw[:, b, 0:ncol],
                        scalar1=self.sN2C[:, rr:rr + 1], scalar2=None,
                        op0=ALU.mult), wait=[m], inc=[(EV, 1)])
                else:
                    self.vec(lambda e, b=b, ncol=ncol: e.tensor_copy(
                        cwb[:, b, 0:ncol], cw[:, b, 0:ncol]),
                        wait=[m], inc=[(EV, 1)])
                ev_done[i] = EV.n
                P.do("gpsimd", None, wait=[(ST, ST.n)])
                P.do("gpsimd", lambda e, b=b, rr=rr, dst=dst, ncol=ncol:
                     e.dma_start(out=dst[rr * 128:(rr + 1) * 128, :],
                                 in_=cwb[:, b, 0:ncol]),
                     wait=[(EV, ev_done[i])], inc=[(ST, 16)])
                st_done[i] = self.ST.n
                i += 1
        P.do("gpsimd", None, wait=[(ST, None)])
        _sc464.close()
        self.bar()

    R_MEAN, R_S1, R_S2, R_VAR, R_SD, R_RSTD, R_NEGMR = 0, 512, 1024, 1536, 2048, 2560, 3072
    R_RSTD1, R_RS, R_M01, R_S3 = 3584, 4096, 4608, 5120

    def emit_ln(self, xq, xsq, lnb, pSum, pSsq, pRt, pBs, bcQ, scr512=None,
                tmpf32=None, want_ffn=False, T=512):
        P, MM, EV, EA = self.P, self.MM, self.EV, self.EA
        R = self.sRow
        self.vec(lambda e: e.tensor_copy(
            lnb[:].rearrange("p c t -> p (c t)"),
            xq[:].rearrange("p c t -> p (c t)")), inc=[(EV, 1)])
        evXB = self.EV.n
        self.act(lambda e: e.activation(
            xsq[:].rearrange("p c t -> p (c t)"),
            xq[:].rearrange("p c t -> p (c t)"), AF.Square), inc=[(EA, 1)])
        eaSQ = self.EA.n
        for dc in range(NCH):
            self.mmv(pSum[0:1, 0:T], self.sOnes[:, 0:1], lnb[:, dc, :],
                     dc == 0, dc == NCH - 1, wait=[(EV, evXB)] if dc == 0 else ())
            self.mmv(pSsq[0:1, 0:T], self.sOnes[:, 0:1], xsq[:, dc, :],
                     dc == 0, dc == NCH - 1,
                     wait=[(EA, eaSQ)] if dc == 0 else (),
                     inc=[(MM, 1)] if (dc == NCH - 1 and not want_ffn) else ())
            if want_ffn:
                self.mmv(pRt[0:1, 0:T], self.sWrC[:, dc:dc + 1], xq[:, dc, :],
                         dc == 0, dc == NCH - 1,
                         inc=[(MM, 1)] if dc == NCH - 1 else ())
        mmStats = self.MM.n
        RM, RS1, RS2, RV = self.R_MEAN, self.R_S1, self.R_S2, self.R_VAR
        RSD, RR, RNM, RR1 = self.R_SD, self.R_RSTD, self.R_NEGMR, self.R_RSTD1
        RRS, RM01, RS3 = self.R_RS, self.R_M01, self.R_S3
        r = lambda o: R[0:1, o:o + T]
        self.vec(lambda e: e.tensor_scalar(out=r(RM), in0=pSum[0:1, 0:T],
                                           scalar1=1.0 / D, scalar2=None,
                                           op0=ALU.mult), wait=[(MM, mmStats)])
        self.vec(lambda e: e.tensor_scalar(out=r(RS1), in0=pSsq[0:1, 0:T],
                                           scalar1=1.0 / D, scalar2=None,
                                           op0=ALU.mult))
        self.vec(lambda e: e.tensor_tensor(out=r(RS2), in0=r(RM), in1=r(RM),
                                           op=ALU.mult))
        self.vec(lambda e: e.tensor_tensor(out=r(RV), in0=r(RS1), in1=r(RS2),
                                           op=ALU.subtract), inc=[(EV, 1)])
        evV = self.EV.n
        self.act(lambda e: e.activation(r(RSD), r(RV), AF.Sqrt, bias=self.sTiny[0:1, 2:3]),
                 wait=[(EV, evV)], inc=[(EA, 1)])
        eaSD = self.EA.n
        self.vec(lambda e: e.reciprocal(r(RR), r(RSD)), wait=[(EA, eaSD)])
        self.vec(lambda e: e.scalar_tensor_tensor(
            out=r(RNM), in0=r(RM), scalar=-1.0, in1=r(RR), op0=ALU.mult,
            op1=ALU.mult), inc=[(EV, 1)])
        if want_ffn:
            self.vec(lambda e: e.tensor_tensor(out=r(RS3), in0=r(RV), in1=r(RR),
                                               op=ALU.mult))
            self.vec(lambda e: e.tensor_tensor(out=r(RS3), in0=r(RS3), in1=r(RR),
                                               op=ALU.mult), inc=[(EV, 1)])
            self.act(lambda e: e.activation(r(RS2), r(RS3), AF.Sqrt,
                                            bias=self.sTiny[0:1, 3:4]),
                     wait=[(EV, None)], inc=[(EA, 1)])
            self.vec(lambda e: e.reciprocal(r(RR1), r(RS2)), wait=[(EA, None)])
            self.vec(lambda e: e.tensor_scalar(
                out=r(RS3), in0=r(RM), scalar1=self.sTiny[0:1, 0:1],
                scalar2=None, op0=ALU.mult))
            self.vec(lambda e: e.tensor_tensor(out=r(RS3), in0=pRt[0:1, 0:T],
                                               in1=r(RS3), op=ALU.subtract))
            self.vec(lambda e: e.tensor_tensor(out=r(RS3), in0=r(RS3),
                                               in1=r(RR), op=ALU.mult))
            self.vec(lambda e: e.tensor_scalar(
                out=r(RRS), in0=r(RS3), scalar1=self.sTiny[0:1, 1:2],
                scalar2=None, op0=ALU.add))
            self.vec(lambda e: e.tensor_single_scalar(r(RM01), r(RRS), 0.0,
                                                      op=ALU.is_gt),
                     inc=[(EV, 1)])
        evRows = self.EV.n
        rows = [RR, RNM, RR1, RM01] if want_ffn else [RR, RNM]
        for i, ro in enumerate(rows):
            self.mmv(pBs[i][:, 0:T], self.sOnes1f[:], R[0:1, ro:ro + T],
                     True, True, wait=[(EV, evRows)] if i == 0 else (),
                     inc=[(MM, 1)] if i == len(rows) - 1 else ())
        mmB = self.MM.n
        for i in range(len(rows)):
            self.vec(lambda e, i=i: e.tensor_copy(bcQ[:, i * 512:i * 512 + T],
                                                  pBs[i][:, 0:T]),
                     wait=[(MM, mmB)] if i == 0 else (),
                     inc=[(EV, 1)] if i == len(rows) - 1 else ())
        evBC = self.EV.n
        bcR = bcQ[:, 0:T]
        bcN = bcQ[:, 512:512 + T]
        if tmpf32 is not None:
            bc3 = lambda c: c[:, None, :].to_broadcast((128, NCH, T))
            self.vec(lambda e: e.tensor_tensor(out=tmpf32[:], in0=xq[:],
                                               in1=bc3(bcR), op=ALU.mult),
                     wait=[(EV, evBC)])
            self.vec(lambda e: e.tensor_tensor(out=lnb[:], in0=tmpf32[:],
                                               in1=bc3(bcN), op=ALU.add),
                     inc=[(EV, 1)])
        else:
            for dc in range(NCH):
                self.vec(lambda e, dc=dc: e.tensor_tensor(
                    out=scr512[:, 0:T], in0=xq[:, dc, :], in1=bcR,
                    op=ALU.mult), wait=[(EV, evBC)] if dc == 0 else ())
                self.vec(lambda e, dc=dc: e.tensor_tensor(
                    out=lnb[:, dc, :], in0=scr512[:, 0:T], in1=bcN,
                    op=ALU.add), inc=[(EV, 1)] if dc == NCH - 1 else ())
        return self.EV.n

    def s1a(self):
        P, LD, ST, MM, EV, EA = self.P, self.LD, self.ST, self.MM, self.EV, self.EA
        nc, I = self.nc, self.inp
        TT = 256
        NT = X9T // TT
        self.sKvB = self.kv_ctx.enter_context(
            self.SB("sKvB", [128, HQ * HD], BF16))
        sc = ExitStack()
        ent = sc.enter_context
        sBkmB = ent(self.SB("sBkmB", [128, MTOT], F32))
        sBvaB = ent(self.SB("sBvaB", [128, D], F32))
        wkmSB = ent(self.SB("wkmSB", [128, NCH, MTOT], BF16))
        wvab = ent(self.SB("wvab", [128, NCH, D], BF16))
        wvld = ent(self.SB("wvld", [128, 2, D], F32))
        xt1 = ent(self.SB("xt1", [128, NCH, TT], F32))
        lnb1 = ent(self.SB("lnb1", [128, NCH, TT], BF16))
        xsq1 = ent(self.SB("xsq1", [128, NCH, TT], BF16))
        scrA = ent(self.SB("scrA", [128, 256], F32))
        zf = ent(self.SB("zf", [128, MTOT], BF16))
        mf = ent(self.SB("mf", [128, MTOT], BF16))
        ef = ent(self.SB("ef", [128, MTOT], BF16))
        kpb = ent(self.SB("kpb", [128, MTOT], BF16))
        vsb = ent(self.SB("vsb", [128, D], BF16))
        kvAcc = ent(self.SB("kvAcc", [64, HQ * HD], F32))
        ksAcc = ent(self.SB("ksAcc", [128, 8], F32))
        bcQ1 = ent(self.SB("bcQ1", [128, 768], F32))
        kvP = ent(self.PS("kvP", [64, HQ * HD], F32))
        pKs = ent(self.PS("pKs", [128, 8], F32))
        T4 = ent(self.PS("T4", [128, 1024], F32))
        T2a = ent(self.PS("T2a", [128, 512], F32))
        self.load(sBkmB[:],
                  self.bnc[MTOT:2 * MTOT][None, :].to_broadcast((128, MTOT)))
        self.load(sBvaB[:], I["bv_a"][None, :].to_broadcast((128, D)))
        self.load(wkmSB[:], fm(self.wkmD))
        ev_c = {}
        for cc in range(NCH):
            b = cc % 2
            if cc >= 2:
                P.do("sync", None, wait=[(EV, ev_c[cc - 2])])
            m = self.load(wvld[:, b, :], I["wvTa"][cc * 128:(cc + 1) * 128, :],
                          s=self.Lp[b])
            self.vec(lambda e, b=b, cc=cc: e.tensor_copy(
                wvab[:, cc, :], wvld[:, b, :]), wait=[m],
                inc=[(EV, 1)])
            ev_c[cc] = EV.n
        self.bar()

        kv_started = False
        kvEvPrev = 0
        mmPrev = 0
        evPrev = 0
        stLn = 0
        vEA = 0
        for t in range(NT):
            P.do("sync", None, wait=[(EV, evPrev), (EV, vEA), (ST, stLn)])
            mx = self.load(xt1[:], fm(I["x9"])[:, :, t * TT:(t + 1) * TT],
                           s=self.Lp[2])
            P.do("vector", None, wait=[(MM, mmPrev), (ST, stLn), mx])
            P.do("scalar", None, wait=[(MM, mmPrev), mx])
            P.do("tensor", None, wait=[(EV, vEA), (EV, evPrev)])
            evLn = self.emit_ln(xt1, xsq1, lnb1, T2a, T4, None,
                                [T4[:, 0:512], T4[:, 512:1024]], bcQ1,
                                scr512=scrA, T=TT)
            if t < EXT // TT:
                self.store(fm(self.lnbuf)[:, :, t * TT:(t + 1) * TT], lnb1[:],
                           wait=[(EV, evLn)])
                stLn = self.ST.n
            if t < 2:
                mmPrev = self.MM.n
                evPrev = evLn
                continue
            last_t = t == NT - 1
            for s in range(2):
                for dc in range(NCH):
                    w = ()
                    if dc == 0:
                        w = [(EV, evLn), (EV, vEA)]
                    self.mmv(T4[:, 0:512], lnb1[:, dc, s * 128:(s + 1) * 128],
                             wkmSB[:, dc, 0:512], dc == 0, dc == NCH - 1,
                             wait=w)
                    self.mmv(T4[:, 512:1024],
                             lnb1[:, dc, s * 128:(s + 1) * 128],
                             wkmSB[:, dc, 512:1024], dc == 0, dc == NCH - 1,
                             inc=[(MM, 1)] if dc == NCH - 1 else ())
                mmK = self.MM.n
                self.vec(lambda e: e.tensor_tensor(out=zf[:], in0=T4[:],
                                                   in1=sBkmB[:], op=ALU.add),
                         wait=[(MM, mmK), (EA, None)])
                self.vec(lambda e: e.tensor_scalar(
                    out=mf[:], in0=zf[:], scalar1=0.0, scalar2=None,
                    op0=ALU.min), inc=[(EV, 1)])
                self.act(lambda e: e.activation(ef[:], mf[:], AF.Exp),
                         wait=[(EV, None)], inc=[(EA, 1)])
                self.vec(lambda e: e.scalar_tensor_tensor(
                    out=kpb[:], in0=zf[:], scalar=0.0, in1=ef[:],
                    op0=ALU.max, op1=ALU.add), wait=[(EA, None)],
                    inc=[(EV, 1)])
                evK = self.EV.n
                first = not kv_started
                for c in range(8):
                    self.mmv(pKs[:, c:c + 1], kpb[:, c * 128:(c + 1) * 128],
                             self.sOnes[:, 0:1], True, True,
                             wait=[(EV, evK), (EV, kvEvPrev)] if c == 0 else ())
                vbanks = [T2a[:, 0:512], T4[:, 0:512], T4[:, 512:1024],
                          T2a[:, 0:512]]
                eaV = {}
                for eg in range(4):
                    pb = vbanks[eg]
                    for dc in range(NCH):
                        w = ()
                        if dc == 0:
                            if eg in (1, 2):
                                w = [(EV, evK)]
                            elif eg == 3:
                                w = [(EV, eaV[0])]
                        self.mmv(pb, lnb1[:, dc, s * 128:(s + 1) * 128],
                                 wvab[:, dc, eg * 512:(eg + 1) * 512],
                                 dc == 0, dc == NCH - 1, wait=w,
                                 inc=[(MM, 1)] if dc == NCH - 1 else ())
                    mmV = self.MM.n
                    self.vec(lambda e, eg=eg, pb=pb: e.tensor_tensor(
                        out=vsb[:, eg * 512:(eg + 1) * 512], in0=pb,
                        in1=sBvaB[:, eg * 512:(eg + 1) * 512], op=ALU.add),
                        wait=[(MM, mmV)], inc=[(EV, 1)])
                    eaV[eg] = self.EV.n
                vEA = self.EV.n
                for hh in range(HQ):
                    self.mmv(kvP[0:64, hh * 128:(hh + 1) * 128],
                             kpb[:, hh * MQ:(hh + 1) * MQ],
                             vsb[:, hh * 128:(hh + 1) * 128],
                             True, True,
                             wait=[(EV, vEA), (EV, kvEvPrev)] if hh == 0 else (),
                             inc=[(MM, 1)] if hh == HQ - 1 else ())
                mmKV = self.MM.n
                if first:
                    self.vec(lambda e: e.tensor_copy(kvAcc[:], kvP[:]),
                             wait=[(MM, mmKV)])
                    self.vec(lambda e: e.tensor_copy(ksAcc[:], pKs[:]),
                             inc=[(EV, 1)])
                else:
                    self.vec(lambda e: e.tensor_tensor(
                        out=kvAcc[:], in0=kvP[:], in1=kvAcc[:], op=ALU.add),
                        wait=[(MM, mmKV)])
                    self.vec(lambda e: e.tensor_tensor(
                        out=ksAcc[:], in0=pKs[:], in1=ksAcc[:], op=ALU.add),
                        inc=[(EV, 1)])
                kvEvPrev = self.EV.n
                kv_started = True
            mmPrev = self.MM.n
            evPrev = self.EV.n
        self.vec(lambda e: e.tensor_copy(self.sKvB[0:64, :], kvAcc[:]),
                 wait=[(EV, None)])
        self.vec(lambda e: e.tensor_copy(self.sKvB[64:128, :], kvAcc[:]))
        self.vec(lambda e: e.tensor_copy(self.sKsumC[:], ksAcc[:]),
                 inc=[(EV, 1)])
        sc.close()
        self.bar()

    def s1b(self):
        P, LD, ST, MM, EV, EA = self.P, self.LD, self.ST, self.MM, self.EV, self.EA
        nc, I = self.nc, self.inp
        _sc762 = ExitStack()
        wqmSB = _sc762.enter_context(self.SB('wqmSB', [128, NCH, MTOT], BF16))
        lnq = _sc762.enter_context(self.SB('lnq', [128, NCH, 512], BF16))
        xq2 = _sc762.enter_context(self.SB('xq2', [128, NCH, 512], F32))
        qsb = _sc762.enter_context(self.SB('qsb', [128, 8, 512], BF16))
        qkb = _sc762.enter_context(self.SB('qkb', [128, 512], BF16))
        zf = _sc762.enter_context(self.SB('zf', [128, 512], F32))
        mf = _sc762.enter_context(self.SB('mf', [128, 512], F32))
        ef = _sc762.enter_context(self.SB('ef', [128, 512], F32))
        recb = _sc762.enter_context(self.SB('recb', [128, 512], F32))
        attnO = _sc762.enter_context(self.SB('attnO', [128, NCH, 512], BF16))
        wol = _sc762.enter_context(self.SB('wol', [128, 2, 1024], BF16))
        PD0 = _sc762.enter_context(self.PS('PD0', [128, 512], F32))
        PD1 = _sc762.enter_context(self.PS('PD1', [128, 512], F32))
        PD2 = _sc762.enter_context(self.PS('PD2', [128, 512], F32))
        PD3 = _sc762.enter_context(self.PS('PD3', [128, 512], F32))
        PD4 = _sc762.enter_context(self.PS('PD4', [128, 512], F32))
        PD5 = _sc762.enter_context(self.PS('PD5', [128, 512], F32))
        PD6 = _sc762.enter_context(self.PS('PD6', [128, 512], F32))
        PD7 = _sc762.enter_context(self.PS('PD7', [128, 512], F32))
        PDs = [PD0, PD1, PD2, PD3, PD4, PD5, PD6, PD7]
        sRowB = _sc762.enter_context(self.SB("sRowB", [1, 8192], F32))
        self.load(wqmSB[:], fm(self.wqmD))
        self.bar()
        evO = 0
        for t in range(5):
            P.do("sync", None, wait=[(EV, None), (ST, None)])
            mln = self.load(lnq[:], fm(self.lnbuf)[:, :, t * 512:(t + 1) * 512], s=self.Lp[2])
            mxx = self.load(xq2[:], fm(I["x9"])[:, :, t * 512:(t + 1) * 512], s=self.Lp[3])
            evQ = {}
            for mc in range(8):
                pq = PDs[mc % 2]
                for dc in range(NCH):
                    w = ()
                    if dc == 0:
                        w = [mln]
                        if mc >= 2:
                            w.append((EV, evQ[mc - 2]))
                        elif t > 0:
                            w.append((EV, evO))
                    self.mmv(pq[:, 0:512], wqmSB[:, dc, mc * 128:(mc + 1) * 128],
                             lnq[:, dc, :], dc == 0, dc == NCH - 1, wait=w,
                             inc=[(MM, 1)] if dc == NCH - 1 else ())
                mmQ = self.MM.n
                self.vec(lambda e, pq=pq, mc=mc: e.tensor_scalar(
                    out=zf[:], in0=pq[:, 0:512],
                    scalar1=self.sBqmC[:, mc:mc + 1], scalar2=None,
                    op0=ALU.add), wait=[(MM, mmQ), (EA, None)])
                self.vec(lambda e: e.tensor_scalar(
                    out=mf[:], in0=zf[:], scalar1=0.0, scalar2=None,
                    op0=ALU.min), inc=[(EV, 1)])
                self.act(lambda e: e.activation(ef[:], mf[:], AF.Exp),
                         wait=[(EV, None)], inc=[(EA, 1)])
                self.vec(lambda e, mc=mc: e.scalar_tensor_tensor(
                    out=qsb[:, mc, :], in0=zf[:], scalar=0.0, in1=ef[:],
                    op0=ALU.max, op1=ALU.add), wait=[(EA, None)],
                    inc=[(EV, 1)])
                evQ[mc] = self.EV.n
            evN = {}
            for c in range(8):
                self.vec(lambda e, c=c: e.tensor_scalar(
                    out=qkb[:], in0=qsb[:, c, :],
                    scalar1=self.sKsumC[:, c:c + 1], scalar2=None,
                    op0=ALU.mult), wait=[(EV, evQ[c])] +
                    ([(MM, None)] if c > 0 else []), inc=[(EV, 1)])
                evqk = self.EV.n
                self.mmv(PD2[0:1, 0:512], self.sOnes[:, 1:2], qkb[:],
                         True, True, wait=[(EV, evqk)])
                self.mmv(PD3[0:1, 0:512], self.sOnes[:, 2:3], qkb[:],
                         True, True, inc=[(MM, 1)])
                mmN = self.MM.n
                self.vec(lambda e, c=c: e.tensor_copy(
                    sRowB[0:1, c * 1024:c * 1024 + 512],
                    PD2[0:1, 0:512]), wait=[(MM, mmN)])
                self.vec(lambda e, c=c: e.tensor_copy(
                    sRowB[0:1, c * 1024 + 512:(c + 1) * 1024],
                    PD3[0:1, 0:512]), inc=[(EV, 1)])
                evN[c] = self.EV.n
            self.vec(lambda e: e.tensor_scalar(
                out=sRowB[0:1, 0:8192], in0=sRowB[0:1, 0:8192],
                scalar1=EPS_LA, scalar2=None, op0=ALU.add),
                wait=[(EV, evN[7])])
            self.vec(lambda e: e.reciprocal(sRowB[0:1, 0:8192],
                                            sRowB[0:1, 0:8192]),
                     inc=[(EV, 1)])
            evR = self.EV.n
            for hh in range(HQ):
                off = (hh // 2) * 1024 + (hh % 2) * 512
                self.mmv(PD4[:, 0:512], self.sOnes1f[:],
                         sRowB[0:1, off:off + 512], True, True,
                         wait=[(EV, evR)] if hh == 0 else [(EV, None)],
                         inc=[(MM, 1)])
                mmB = self.MM.n
                self.act(lambda e: e.activation(recb[:], PD4[:, 0:512],
                                                AF.Copy),
                         wait=[(MM, mmB), (EV, None)], inc=[(EA, 1)])
                eaB = self.EA.n
                base = (hh % 2) * 64
                self.mmv(PD5[:, 0:512],
                         self.sKvB[base:base + 64, hh * 128:(hh + 1) * 128],
                         qsb[base:base + 64, hh // 2, :],
                         True, True, wait=[(EV, None)], inc=[(MM, 1)])
                mmU = self.MM.n
                self.vec(lambda e, hh=hh: e.tensor_tensor(
                    out=attnO[:, hh, :], in0=PD5[:, 0:512], in1=recb[:],
                    op=ALU.mult), wait=[(MM, mmU), (EA, eaB)],
                    inc=[(EV, 1)])
            evAO = self.EV.n
            P.do("sync", None, wait=[mxx])
            for eh in range(2):
                ev_wol = {}
                for cc in range(NCH):
                    b = cc % 2
                    if cc >= 2:
                        P.do("sync", None, wait=[(MM, ev_wol[cc - 2])])
                    else:
                        P.do("sync", None, wait=[(MM, MM.n)])
                    mwo = self.load(wol[:, b, :],
                                    self.woTaD[cc * 128:(cc + 1) * 128,
                                               eh * 1024:(eh + 1) * 1024], s=self.Lp[b])
                    for e8 in range(8):
                        w = [mwo]
                        if cc == 0:
                            w = [mwo, (EV, None)]
                        self.mmv(PDs[e8][:, 0:512],
                                 wol[:, b, e8 * 128:(e8 + 1) * 128],
                                 attnO[:, cc, :], cc == 0, cc == NCH - 1,
                                 wait=w if (e8 == 0 or cc == 0) else (),
                                 inc=[(MM, 1)] if e8 == 7 else ())
                    ev_wol[cc] = self.MM.n
                mmO = self.MM.n
                for e8 in range(8):
                    ec = eh * 8 + e8
                    self.vec(lambda e, e8=e8, ec=ec: e.scalar_tensor_tensor(
                        out=xq2[:, ec, :], in0=PDs[e8][:, 0:512],
                        scalar=self.sBoC[:, ec:ec + 1],
                        in1=xq2[:, ec, :], op0=ALU.add, op1=ALU.add),
                        wait=[(MM, mmO)] if e8 == 0 else (),
                        inc=[(EV, 1)] if e8 == 7 else ())
                evO = self.EV.n
            self.store(fm(self.xr)[:, :, t * 512:(t + 1) * 512], xq2[:],
                       wait=[(EV, evO)])
            P.do("sync", None, wait=[(ST, None)])
        _sc762.close()
        self.kv_ctx.close()
        self.bar()

    def ffn_quarter(self, src3d, dst3d, tiles, first_pass):
        P, LD, ST, MM, EV, EA = self.P, self.LD, self.ST, self.MM, self.EV, self.EA
        nc = self.nc
        NFC = FF // 128
        _sc915 = ExitStack()
        fxq = _sc915.enter_context(self.SB('fxq', [128, NCH, 512], F32))
        fh = _sc915.enter_context(self.SB('fh', [128, NCH, 512], BF16))
        fln = _sc915.enter_context(self.SB('fln', [128, NCH, 512], BF16))
        fR5 = _sc915.enter_context(self.SB('fR5', [128, NCH, 512], BF16))
        fR6 = _sc915.enter_context(self.SB('fR6', [128, NCH, 512], BF16))
        fG = _sc915.enter_context(self.SB('fG', [128, NFC, 512], BF16))
        fbcQ = _sc915.enter_context(self.SB('fbcQ', [128, 2560], F32))
        fscr = _sc915.enter_context(self.SB('fscr', [128, 512], F32))
        fgl = _sc915.enter_context(self.SB('fgl', [128, 512], BF16))
        wsb = _sc915.enter_context(self.SB('wsb', [128, 4, NCH * 128], BF16))
        QD0 = _sc915.enter_context(self.PS('QD0', [128, 512], F32))
        QD1 = _sc915.enter_context(self.PS('QD1', [128, 512], F32))
        QD2 = _sc915.enter_context(self.PS('QD2', [128, 512], F32))
        QD3 = _sc915.enter_context(self.PS('QD3', [128, 512], F32))
        QD4 = _sc915.enter_context(self.PS('QD4', [128, 512], F32))
        QD5 = _sc915.enter_context(self.PS('QD5', [128, 512], F32))
        QD6 = _sc915.enter_context(self.PS('QD6', [128, 512], F32))
        QD7 = _sc915.enter_context(self.PS('QD7', [128, 512], F32))
        QDs = [QD0, QD1, QD2, QD3, QD4, QD5, QD6, QD7]
        stPrev = 0
        mmPrev = 0
        evPrev = 0
        for ti, tq in enumerate(tiles):
            P.do("sync", None, wait=[(ST, stPrev), (EV, evPrev)])
            mx = self.load(fxq[:], src3d(tq), s=self.Lp[2])
            P.do("vector", None, wait=[mx, (MM, mmPrev),
                                       (ST, stPrev)])
            P.do("scalar", None, wait=[mx, (MM, mmPrev)])
            P.do("tensor", None, wait=[(EV, evPrev)])
            evLn = self.emit_ln(fxq, fR6, fln, QD6, QD7, QD5,
                                [QD0, QD1, QD2, QD3], fbcQ, scr512=fscr,
                                want_ffn=True)
            bc3 = lambda off: fbcQ[:, off:off + 512][:, None, :] \
                .to_broadcast((128, NCH, 512))
            self.vec(lambda e: e.tensor_tensor(
                out=fR6[:], in0=fln[:], in1=bc3(1024), op=ALU.mult),
                inc=[(EV, 1)])
            evH1 = self.EV.n
            evB = {}
            for ep in range(NCH // 2):
                g = ep % 2
                if ep >= 2:
                    P.do("sync", None, wait=[(MM, evB[("mm", 2 * ep - 3)])])
                else:
                    P.do("sync", None, wait=[(MM, MM.n)])
                wb3 = wsb[:, g * 2:g * 2 + 2, :].rearrange(
                    "p a f -> p (a f)").rearrange("p (c e) -> p c e", e=256)
                mw = self.load(wb3,
                               fm(self.W2D)[:, :, ep * 256:(ep + 1) * 256],
                               s=self.Lp[g])
                for fi in range(2):
                    ec = ep * 2 + fi
                    pq = QDs[ec % 2]
                    for dc in range(NCH):
                        w = ()
                        if dc == 0:
                            w = [mw, (EV, evH1)]
                            if ec >= 2:
                                w.append((EV, evB[ec - 2]))
                        self.mmv(pq[:, 0:512],
                                 wb3[:, dc, fi * 128:(fi + 1) * 128],
                                 fR6[:, dc, :], dc == 0, dc == NCH - 1,
                                 wait=w,
                                 inc=[(MM, 1)] if dc == NCH - 1 else ())
                    evB[("mm", ec)] = self.MM.n
                    self.vec(lambda e, ec=ec, pq=pq: e.scalar_tensor_tensor(
                        out=fh[:, ec, :], in0=pq[:, 0:512],
                        scalar=self.sBW2C[:, ec:ec + 1], in1=fln[:, ec, :],
                        op0=ALU.add, op1=ALU.add),
                        wait=[(MM, evB[("mm", ec)])], inc=[(EV, 1)])
                    evB[ec] = self.EV.n
            evHdone = self.EV.n
            self.act(lambda e: e.activation(
                fR6[:].rearrange("p c t -> p (c t)"),
                fh[:].rearrange("p c t -> p (c t)"), AF.Square),
                wait=[(EV, evHdone)], inc=[(EA, 1)])
            eaH = self.EA.n
            for dc in range(NCH):
                self.mmv(QD6[0:1, 0:512], self.sOnes[:, 0:1], fR6[:, dc, :],
                         dc == 0, dc == NCH - 1,
                         wait=[(EA, eaH)] if dc == 0 else (),
                         inc=[(MM, 1)] if dc == NCH - 1 else ())
            mmH = self.MM.n
            r = lambda o: self.sRow[0:1, o:o + 512]
            self.vec(lambda e: e.tensor_scalar(
                out=r(0), in0=QD6[0:1, 0:512], scalar1=1.0 / D,
                scalar2=None, op0=ALU.mult), wait=[(MM, mmH)],
                inc=[(EV, 1)])
            self.act(lambda e: e.activation(r(512), r(0), AF.Sqrt,
                                            bias=self.sTiny[0:1, 3:4]),
                     wait=[(EV, None)], inc=[(EA, 1)])
            self.vec(lambda e: e.reciprocal(r(1024), r(512)),
                     wait=[(EA, None)], inc=[(EV, 1)])
            self.mmv(QD4[:, 0:512], self.sOnes1f[:], r(1024), True, True,
                     wait=[(EV, None)], inc=[(MM, 1)])
            mmB2 = self.MM.n
            self.vec(lambda e: e.tensor_copy(fbcQ[:, 2048:2560],
                                             QD4[:, 0:512]),
                     wait=[(MM, mmB2)])
            self.vec(lambda e: e.tensor_tensor(
                out=fR5[:], in0=fh[:],
                in1=fbcQ[:, 2048:2560][:, None, :]
                .to_broadcast((128, NCH, 512)),
                op=ALU.mult), inc=[(EV, 1)])
            evH2 = self.EV.n
            evG = {}
            eaG = {}
            mmBlast = self.MM.n
            for ft in range(NFC):
                b = ft % 2
                pg = QDs[2 + (ft % 2) * 2]
                pu = QDs[3 + (ft % 2) * 2]
                if ft < 2:
                    P.do("sync", None, wait=[(MM, mmBlast)])
                else:
                    P.do("sync", None, wait=[(MM, evG[("mm", ft - 2)])])
                mg = self.load(
                    wsb[:, b, :].rearrange("p (c e) -> p c e", e=128),
                    fm(self.WgD)[:, :, ft * 128:(ft + 1) * 128], s=self.Lp[b])
                mu = self.load(
                    wsb[:, b + 2, :].rearrange("p (c e) -> p c e", e=128),
                    fm(self.WuD)[:, :, ft * 128:(ft + 1) * 128],
                    s=self.Lp[2 + b])
                for dc in range(NCH):
                    w = ()
                    if dc == 0:
                        w = [mg, mu, (EV, evH2)]
                        if ft >= 2:
                            w.append((EV, evG[ft - 2]))
                            w.append((EA, eaG[ft - 2]))
                    self.mmv(pg[:, 0:512],
                             wsb[:, b, dc * 128:(dc + 1) * 128],
                             fR5[:, dc, :], dc == 0, dc == NCH - 1, wait=w)
                    self.mmv(pu[:, 0:512],
                             wsb[:, b + 2, dc * 128:(dc + 1) * 128],
                             fR5[:, dc, :], dc == 0, dc == NCH - 1,
                             inc=[(MM, 1)] if dc == NCH - 1 else ())
                evG[("mm", ft)] = self.MM.n
                self.act(lambda e, pg=pg: e.activation(fgl[:], pg[:, 0:512],
                                                       AF.Silu),
                         wait=[(MM, evG[("mm", ft)]), (EV, None)],
                         inc=[(EA, 1)])
                eaG[ft] = self.EA.n
                self.vec(lambda e, ft=ft, pu=pu: e.tensor_tensor(
                    out=fG[:, ft, :], in0=pu[:, 0:512], in1=fgl[:],
                    op=ALU.mult), wait=[(EA, None)], inc=[(EV, 1)])
                evG[ft] = self.EV.n
            evGall = self.EV.n
            evE = 0
            mmGlast = self.MM.n
            FTC = 4
            for eh in range(2):
                mmD = {}
                for fc in range(NFC // FTC):
                    g = fc % 2
                    if fc < 2:
                        P.do("sync", None, wait=[(MM, mmGlast if eh == 0
                                                  else MM.n)])
                    else:
                        P.do("sync", None, wait=[(MM, mmD[fc - 2])])
                    wv3 = wsb[:, g * 2:g * 2 + 2, :].rearrange(
                        "p c (b e) -> p (c b) e", e=1024)
                    md = self.load(
                        wv3,
                        self.WdD[fc * FTC * 128:(fc + 1) * FTC * 128,
                                 eh * 1024:(eh + 1) * 1024]
                        .rearrange("(a p) e -> p a e", p=128),
                        s=self.Lp[g])
                    for fi in range(FTC):
                        ft = fc * FTC + fi
                        for e8 in range(8):
                            w = [md] if (e8 == 0 and fi == 0) else ()
                            if ft == 0 and e8 == 0:
                                w = [md, (EV, evGall), (EA, None)]
                                if eh == 1:
                                    w.append((EV, evE))
                            self.mmv(QDs[e8][:, 0:512],
                                     wv3[:, fi, e8 * 128:(e8 + 1) * 128],
                                     fG[:, ft, :], ft == 0, ft == NFC - 1,
                                     wait=w,
                                     inc=[(MM, 1)] if e8 == 7 else ())
                    mmD[fc] = self.MM.n
                mmE = self.MM.n
                for e8 in range(8):
                    ec = eh * 8 + e8
                    self.vec(lambda e, e8=e8, ec=ec: e.tensor_tensor(
                        out=fh[:, ec, :], in0=QDs[e8][:, 0:512],
                        in1=fh[:, ec, :], op=ALU.add),
                        wait=[(MM, mmE)] if e8 == 0 else (),
                        inc=[(EV, 1)] if e8 == 7 else ())
                evE = self.EV.n
            m01b3 = fbcQ[:, 1536:2048][:, None, :].to_broadcast(
                (128, NCH, 512))
            self.vec(lambda e: e.tensor_tensor(out=fR6[:], in0=fh[:],
                                               in1=fln[:],
                                               op=ALU.subtract),
                     wait=[(EV, evE)])
            self.vec(lambda e: e.tensor_tensor(out=fR5[:], in0=fR6[:],
                                               in1=m01b3, op=ALU.mult))
            self.vec(lambda e: e.tensor_tensor(out=fh[:], in0=fR5[:],
                                               in1=fln[:], op=ALU.add))
            self.vec(lambda e: e.tensor_tensor(out=fxq[:], in0=fh[:],
                                               in1=fxq[:], op=ALU.add),
                     inc=[(EV, 1)])
            evFin = self.EV.n
            P.do("gpsimd", None, wait=[(EV, evFin)])
            self.store(dst3d(tq), fxq[:])
            stPrev = self.ST.n
            mmPrev = self.MM.n
            evPrev = evFin
        _sc915.close()
        self.bar()

    def s2(self):
        src = lambda tq: fm(self.xr)[:, :, tq * 512:(tq + 1) * 512]
        dst = lambda tq: fm(self.xr)[:, :, tq * 512:(tq + 1) * 512]
        self.ffn_quarter(src, dst, [0, 1, 2, 3, 4], True)

    def s4(self):
        src = lambda tq: fm(self.xr)[:, :, HALO + tq * 512:HALO + (tq + 1) * 512]
        dst = lambda tq: fm(self.out)[:, :, tq * 512:(tq + 1) * 512]
        self.ffn_quarter(src, dst, [0, 1, 2, 3], False)

    def s3(self):
        P, LD, ST, MM, EV, EA = self.P, self.LD, self.ST, self.MM, self.EV, self.EA
        nc, I = self.nc, self.inp
        _sc1121 = ExitStack()
        xt3 = _sc1121.enter_context(self.SB('xt3', [128, NCH, 512], F32))
        lnb3 = _sc1121.enter_context(self.SB('lnb3', [128, NCH, 512], BF16))
        xsq3 = _sc1121.enter_context(self.SB('xsq3', [128, NCH, 512], BF16))
        scr3 = _sc1121.enter_context(self.SB('scr3', [128, 512], F32))
        bcQ3 = _sc1121.enter_context(self.SB('bcQ3', [128, 1024], F32))
        pA3 = _sc1121.enter_context(self.PS('pA3', [128, 512], F32))
        pB3 = _sc1121.enter_context(self.PS('pB3', [128, 1024], F32))
        mmPrev = 0
        stPrev = 0
        evPrev = 0
        for t in range(5):
            P.do("sync", None, wait=[(EV, evPrev), (ST, stPrev),
                                     (EA, None)])
            mx = self.load(xt3[:], fm(self.xr)[:, :, t * 512:(t + 1) * 512], s=self.Lp[2])
            P.do("vector", None, wait=[mx, (MM, mmPrev)])
            P.do("scalar", None, wait=[mx, (MM, mmPrev)])
            P.do("tensor", None, wait=[(EV, evPrev)])
            evPrev = self.emit_ln(xt3, xsq3, lnb3, pA3, pB3, None,
                                  [pB3[:, 0:512], pB3[:, 512:1024]], bcQ3,
                                  scr512=scr3)
            self.store(fm(self.lnbuf)[:, :, t * 512:(t + 1) * 512], lnb3[:],
                       wait=[(EV, evPrev)])
            stPrev = self.ST.n
            mmPrev = self.MM.n
        _sc1121.close()
        self.bar()

        _sc1146 = ExitStack()
        QT = _sc1146.enter_context(self.SB('QT', [128, NCH, OWN], BF16))
        lnq3 = _sc1146.enter_context(self.SB('lnq3', [128, NCH, 512], BF16))
        _sc1151 = ExitStack()
        WqS = _sc1151.enter_context(self.SB('WqS', [128, NCH, D], BF16))
        wld3 = _sc1151.enter_context(self.SB('wld3', [128, 2, D], F32))
        pQ3a = _sc1151.enter_context(self.PS('pQ3a', [128, 512], F32))
        pQ3b = _sc1151.enter_context(self.PS('pQ3b', [128, 512], F32))
        ev_c = {}
        for cc in range(NCH):
            b = cc % 2
            if cc >= 2:
                P.do("sync", None, wait=[(EV, ev_c[cc - 2])])
            m = self.load(wld3[:, b, :],
                          I["Wq_b"][cc * 128:(cc + 1) * 128, :],
                          s=self.Lp[b])
            self.vec(lambda e, b=b, cc=cc: e.tensor_copy(
                WqS[:, cc, :], wld3[:, b, :]), wait=[m],
                inc=[(EV, 1)])
            ev_c[cc] = EV.n
        evW = EV.n
        for tt in range(4):
            P.do("sync", None, wait=[(EV, None)])
            mln = self.load(lnq3[:],
                            fm(self.lnbuf)[:, :, HALO + tt * 512:
                                           HALO + (tt + 1) * 512], s=self.Lp[2])
            evQ = {}
            for qc in range(NCH):
                pq = pQ3a if qc % 2 == 0 else pQ3b
                for dc in range(NCH):
                    w = ()
                    if dc == 0:
                        w = [mln, (EV, evW)]
                        if qc >= 2:
                            w.append((EV, evQ[qc - 2]))
                    self.mmv(pq[:, 0:512],
                             WqS[:, dc, qc * 128:(qc + 1) * 128],
                             lnq3[:, dc, :], dc == 0, dc == NCH - 1,
                             wait=w,
                             inc=[(MM, 1)] if dc == NCH - 1 else ())
                mmQ = self.MM.n
                self.vec(lambda e, qc=qc, tt=tt, pq=pq: e.tensor_scalar(
                    out=QT[:, qc, tt * 512:(tt + 1) * 512],
                    in0=pq[:, 0:512], scalar1=1.0 / HD, scalar2=None,
                    op0=ALU.mult), wait=[(MM, mmQ)], inc=[(EV, 1)])
                evQ[qc] = self.EV.n
        _sc1151.close()
        self.bar()

        KT = _sc1146.enter_context(self.SB('KT', [128, HKV, EXT], BF16))
        VT = _sc1146.enter_context(self.SB('VT', [128, EXT // 128, 512], BF16))
        _sc1195 = ExitStack()
        WkS = _sc1195.enter_context(self.SB('WkS', [128, NCH, 512], BF16))
        WvS = _sc1195.enter_context(self.SB('WvS', [128, NCH, 512], BF16))
        wld4 = _sc1195.enter_context(self.SB('wld4', [128, 2, 512], F32))
        pK3a = _sc1195.enter_context(self.PS('pK3a', [128, 512], F32))
        pK3b = _sc1195.enter_context(self.PS('pK3b', [128, 512], F32))
        ev_c2 = {}
        for i in range(2 * NCH):
            mat, dstb = (I["Wk_b"], WkS) if i % 2 == 0 else \
                        (I["Wv_b"], WvS)
            cc = i // 2
            b = i % 2
            if i >= 2:
                P.do("sync", None, wait=[(EV, ev_c2[i - 2])])
            m = self.load(wld4[:, b, :],
                          mat[cc * 128:(cc + 1) * 128, :], s=self.Lp[b])
            self.vec(lambda e, b=b, cc=cc, dstb=dstb: e.tensor_copy(
                dstb[:, cc, :], wld4[:, b, :]), wait=[m],
                inc=[(EV, 1)])
            ev_c2[i] = EV.n
        evW2 = EV.n
        for tt in range(5):
            P.do("sync", None, wait=[(EV, None), (EA, None)])
            mln = self.load(lnq3[:],
                            fm(self.lnbuf)[:, :, tt * 512:(tt + 1) * 512], s=self.Lp[2])
            prevK = {}
            for gc in range(HKV):
                pq = pK3a if gc % 2 == 0 else pK3b
                for dc in range(NCH):
                    w = ()
                    if dc == 0:
                        w = [mln, (EV, evW2), (EA, None)]
                    self.mmv(pq[:, 0:512],
                             WkS[:, dc, gc * 128:(gc + 1) * 128],
                             lnq3[:, dc, :], dc == 0, dc == NCH - 1,
                             wait=w,
                             inc=[(MM, 1)] if dc == NCH - 1 else ())
                mmK = self.MM.n
                self.act(lambda e, gc=gc, tt=tt, pq=pq: e.activation(
                    KT[:, gc, tt * 512:(tt + 1) * 512], pq[:, 0:512],
                    AF.Copy), wait=[(MM, mmK)], inc=[(EA, 1)])
            for sub in range(4):
                pq = pK3a if sub % 2 == 0 else pK3b
                for dc in range(NCH):
                    w = ()
                    if dc == 0:
                        w = [(EA, None), (EV, None)]
                    self.mmv(pq[:, 0:512],
                             lnq3[:, dc, sub * 128:(sub + 1) * 128],
                             WvS[:, dc, :], dc == 0, dc == NCH - 1,
                             wait=w,
                             inc=[(MM, 1)] if dc == NCH - 1 else ())
                mmV = self.MM.n
                self.vec(lambda e, sub=sub, tt=tt, pq=pq: e.tensor_copy(
                    VT[:, tt * 4 + sub, :], pq[:, 0:512]),
                    wait=[(MM, mmV)], inc=[(EV, 1)])
        _sc1195.close()
        self.bar()

        _sc1253 = ExitStack()
        Ssb = _sc1253.enter_context(self.SB('Ssb', [128, 640], F32))
        Pn = _sc1253.enter_context(self.SB('Pn', [128, 640], BF16))
        PTs = _sc1253.enter_context(self.SB('PTs', [128, 5, 128], BF16))
        nmx = _sc1253.enter_context(self.SB('nmx', [128, 2], F32))
        aO = _sc1253.enter_context(self.SB('aO', [128, NCH, 512], BF16))
        xq3b = _sc1253.enter_context(self.SB('xq3b', [128, NCH, 512], F32))
        wob3 = _sc1253.enter_context(self.SB('wob3', [128, 2, 512], BF16))
        pScA = _sc1253.enter_context(self.PS('pScA', [128, 640], F32))
        pScB = _sc1253.enter_context(self.PS('pScB', [128, 640], F32))
        pPT = _sc1253.enter_context(self.PS('pPT', [128, 640], BF16))
        pO = _sc1253.enter_context(self.PS('pO', [128, 512], F32))
        pX = _sc1253.enter_context(self.PS('pX', [128, 512], F32))
        pY = _sc1253.enter_context(self.PS('pY', [128, 512], F32))
        sMaskR = _sc1253.enter_context(self.SB("sMaskR", [128, 640], BF16))
        sMaskF = _sc1253.enter_context(self.SB("sMaskF", [128, 4, 640], BF16))
        self.load(sMaskR[:], I["maskR"])
        self.load(sMaskF[:], I["maskF"].rearrange("(j p) k -> p j k", p=128))
        P.do("vector", None, wait=[(self.LD, None)])
        stPrev = 0
        evMA = {}
        uix = 0
        for j3 in range(4):
            P.do("sync", None, wait=[(EV, None), (ST, stPrev)])
            mxj = self.load(xq3b[:],
                            fm(self.xr)[:, :, HALO + j3 * 512:
                                        HALO + (j3 + 1) * 512],
                            s=self.Lp[3])
            evAO = 0
            for hh in range(HQ):
                g = hh // 4
                for qt in range(4):
                    koff = j3 * 512 + qt * 128
                    pS = pScA if uix % 2 == 0 else pScB
                    wS = [(EV, evMA.get(uix - 2, 0))]
                    self.mmv(pS[:, 0:512],
                             QT[:, hh, j3 * 512 + qt * 128:
                                j3 * 512 + (qt + 1) * 128],
                             KT[:, g, koff:koff + 512], True, True, wait=wS)
                    self.mmv(pS[:, 512:640],
                             QT[:, hh, j3 * 512 + qt * 128:
                                j3 * 512 + (qt + 1) * 128],
                             KT[:, g, koff + 512:koff + 640],
                             True, True, inc=[(MM, 1)])
                    mmS = self.MM.n
                    msk = (sMaskF[:, qt, :] if j3 == 0 else sMaskR[:])
                    self.vec(lambda e, msk=msk, pS=pS: e.tensor_tensor(
                        out=Ssb[:], in0=pS[:], in1=msk, op=ALU.add),
                        wait=[(MM, mmS), (EA, None)])
                    evMA[uix] = self.EV.n
                    self.vec(lambda e: e.reduce_max(
                        nmx[:, 0:1], Ssb[:], axis=AXX, negate=True),
                        inc=[(EV, 1)])
                    evM = self.EV.n
                    self.act(lambda e: e.activation(
                        Pn[:], Ssb[:], AF.Exp, bias=nmx[:, 0:1],
                        accum_out=nmx[:, 1:2]),
                        wait=[(EV, evM), (MM, None)],
                        inc=[(EA, 1)])
                    eaE = self.EA.n
                    self.vec(lambda e: e.reciprocal(
                        nmx[:, 1:2], nmx[:, 1:2]), wait=[(EA, eaE)])
                    self.vec(lambda e: e.tensor_scalar(
                        out=Pn[:], in0=Pn[:], scalar1=nmx[:, 1:2],
                        scalar2=None, op0=ALU.mult), inc=[(EV, 1)])
                    evP = self.EV.n
                    for s5 in range(5):
                        self.P.do("tensor", lambda e, s5=s5: e.transpose(
                            pPT[:, s5 * 128:(s5 + 1) * 128],
                            Pn[:, s5 * 128:(s5 + 1) * 128],
                            self.sIDN[:]),
                            wait=[(EV, evP)] if s5 == 0 else (),
                            inc=[(MM, 1)] if s5 == 4 else ())
                    mmT = self.MM.n
                    self.vec(lambda e: e.tensor_copy(
                        PTs[:].rearrange("p a b -> p (a b)"), pPT[:]),
                        wait=[(MM, mmT)], inc=[(EV, 1)])
                    evPT = self.EV.n
                    for s5 in range(5):
                        rsub = j3 * 4 + qt + s5
                        self.mmv(pO[:, qt * 128:(qt + 1) * 128],
                                 VT[:, rsub, g * 128:(g + 1) * 128],
                                 PTs[:, s5, :],
                                 s5 == 0, s5 == 4,
                                 wait=[(EV, evPT), (EA, None)]
                                 if s5 == 0 else (),
                                 inc=[(MM, 1)] if s5 == 4 else ())
                    uix += 1
                mmPV = self.MM.n
                self.act(lambda e, hh=hh: e.activation(
                    aO[:, hh, :], pO[:, 0:512], AF.Copy),
                    wait=[(MM, mmPV)], inc=[(EA, 1)])
                evAO = self.EA.n
            pb3 = [pO, pX, pY]
            for sw in range(6):
                e0 = sw * 3
                ne = min(3, NCH - e0)
                mmWO = {}
                for cc in range(NCH):
                    b = cc % 2
                    if cc >= 2:
                        P.do("sync", None, wait=[(MM, mmWO[cc - 2])])
                    else:
                        P.do("sync", None, wait=[(MM, MM.n)])
                    mwo = self.load(wob3[:, b, 0:ne * 128],
                                    self.WobD[cc * 128:(cc + 1) * 128,
                                              e0 * 128:(e0 + ne) * 128],
                                    s=self.Lp[b])
                    for e4 in range(ne):
                        w = [mwo]
                        if cc == 0:
                            w = [mwo, (EA, evAO), (EV, None)]
                        self.mmv(pb3[e4][:, 0:512],
                                 wob3[:, b, e4 * 128:(e4 + 1) * 128],
                                 aO[:, cc, :], cc == 0, cc == NCH - 1,
                                 wait=w if (e4 == 0 or cc == 0) else (),
                                 inc=[(MM, 1)] if e4 == ne - 1 else ())
                    mmWO[cc] = self.MM.n
                mmO = self.MM.n
                for e4 in range(ne):
                    ec = e0 + e4
                    self.vec(lambda e, e4=e4, ec=ec: e.tensor_tensor(
                        out=xq3b[:, ec, :], in0=pb3[e4][:, 0:512],
                        in1=xq3b[:, ec, :], op=ALU.add),
                        wait=[(MM, mmO), mxj] if e4 == 0 else (),
                        inc=[(EV, 1)] if e4 == ne - 1 else ())
            self.store(fm(self.xr)[:, :, HALO + j3 * 512:
                                   HALO + (j3 + 1) * 512], xq3b[:],
                       wait=[(EV, EV.n)])
            stPrev = self.ST.n
        P.do("sync", None, wait=[(ST, None)])
        _sc1253.close()
        _sc1146.close()
        self.bar()




def _band_mask():
    qi = np.arange(128)[:, None]
    ki = np.arange(640)[None, :]
    ok = (ki >= qi + 1) & (ki <= qi + WIN)
    return np.where(ok, 0.0, NEG).astype(np.float32)


WIN = 512


def _first_masks(h):
    base = _band_mask()
    out = np.zeros((4, 128, 640), np.float32)
    for qt in range(4):
        m = base.copy()
        if h == 0:
            ki = np.arange(640)[None, :]
            m = np.where(qt * 128 + ki >= 512, m, NEG)
        out[qt] = m
    return out.reshape(512, 640).astype(np.float32)


def make_inputs(core, kw):
    b, h = core // 2, core % 2
    x = np.asarray(kw["L0_attn_a_hidden_states"], np.float32)
    own = x[b, h * OWN:(h + 1) * OWN]
    other = x[b, (1 - h) * OWN:(2 - h) * OWN]
    halo = (x[b, OWN - HALO:OWN] if h == 1
            else np.zeros((HALO, D), np.float32))
    ext9 = np.concatenate([halo, own, other], axis=0)
    f32 = lambda a: np.ascontiguousarray(np.asarray(a, np.float32))
    d = {
        "x9": f32(ext9.T),
        "wq_a": f32(kw["L0_attn_a_wq"]),
        "wk_a": f32(kw["L0_attn_a_wk"]),
        "wvTa": f32(np.asarray(kw["L0_attn_a_wv"], np.float32).T),
        "woTa": f32(np.asarray(kw["L0_attn_a_wo"], np.float32).T),
        "wmq": f32(kw["L0_attn_a_wmap_q"]),
        "wmk": f32(kw["L0_attn_a_wmap_k"]),
        "bq_a": f32(kw["L0_attn_a_bq"]),
        "bk_a": f32(kw["L0_attn_a_bk"]),
        "bv_a": f32(kw["L0_attn_a_bv"]),
        "bo_a": f32(kw["L0_attn_a_bo"]),
        "Wq_b": f32(kw["L0_attn_b_Wq"]),
        "Wk_b": f32(kw["L0_attn_b_Wk"]),
        "Wv_b": f32(kw["L0_attn_b_Wv"]),
        "Wo_b": f32(kw["L0_attn_b_Wo"]),
        "wvTf": f32(np.asarray(kw["L0_ffn_wv"], np.float32).T),
        "wo_f": f32(kw["L0_ffn_wo"]),
        "w_r": f32(kw["L0_ffn_w_router"]),
        "b_r": f32(np.asarray(kw["L0_ffn_b_router"], np.float32).reshape(1)),
        "n1w": f32(kw["L0_ffn_norm1_w"]),
        "n2w": f32(kw["L0_ffn_norm2_w"]),
        "bv_f": f32(kw["L0_ffn_bv"]),
        "Wg": f32(kw["L0_ffn_w_gate"]),
        "Wu": f32(kw["L0_ffn_w_up"]),
        "Wd": f32(kw["L0_ffn_w_down"]),
        "maskR": _band_mask().astype(ml_dtypes.bfloat16),
        "maskF": _first_masks(h).astype(ml_dtypes.bfloat16),
        "IDN": np.eye(128, dtype=np.float32).astype(ml_dtypes.bfloat16),
        "IDNF": np.eye(128, dtype=np.float32),
        "onesb": np.stack([np.ones(128),
                           (np.arange(128) < 64).astype(np.float32),
                           (np.arange(128) >= 64).astype(np.float32),
                           np.zeros(128)], axis=1).astype(ml_dtypes.bfloat16),
        "ones1f": np.ones((1, 128), np.float32),
        "epsv": np.array([[EPS_LN, EPS_RMS]], np.float32),
    }
    return d


def build(upto=None, dev_taps=()):
    B = Builder()
    phases = [("p0a", B.p0a), ("p0b", B.p0b), ("p0c", B.p0c),
              ("s1a", B.s1a), ("s1b", B.s1b), ("s2", B.s2),
              ("s3", B.s3), ("s4", B.s4)]
    for name, fn in phases:
        fn()
        if name == upto:
            break
    for tname in dev_taps:
        ap, dt, shape = B.taps[tname]
        o = B.nc.dram_tensor("tap_" + tname, shape, dt,
                             kind="ExternalOutput").ap()
        B.P.do("gpsimd", None, wait=[(B.ST, B.ST.n)])
        B.P.do("gpsimd", lambda e, o=o, ap=ap: e.dma_start(out=o, in_=ap),
               inc=[(B.ST, 16)])
    B.P.do("gpsimd", None, wait=[(B.ST, None)])
    B.P.do("sync", None, wait=[(B.ST, None), (B.LD, None)])
    with B.nc.Block() as block:
        B.P.replay(block)
    B.kv_ctx.close()
    B.ctx.close()
    return B.nc


def kernel(**inputs):
    nc = build()
    in_maps = [make_inputs(c, inputs) for c in range(8)]
    res = run_bass_kernel_spmd(nc, in_maps, core_ids=list(range(8)))
    x = np.zeros((4, FULL, D), np.float32)
    for c in range(8):
        b, h = c // 2, c % 2
        o = np.asarray(res.results[c]["out"])
        x[b, h * OWN:(h + 1) * OWN, :] = o.T
    return x


# version 06
# speedup vs baseline: 1.1088x; 1.1088x over previous
import numpy as np
import ml_dtypes
from contextlib import ExitStack

import concourse.bass as bass
import concourse.mybir as mybir
from concourse.bass_utils import run_bass_kernel_spmd

F32 = mybir.dt.float32
BF16 = mybir.dt.bfloat16
AF = mybir.ActivationFunctionType
ALU = mybir.AluOpType
AXX = mybir.AxisListType.X

D = 2048
NCH = 16
OWN = 2048
HALO = 512
EXT = OWN + HALO
FULL = 4096
X9T = HALO + FULL
FF = 5632
HQ, HKV, HD = 16, 4, 128
MQ = 64
MTOT = HQ * MQ
NEG = -1.0e30
EPS_LN = 1.0e-5
EPS_RMS = 1.0e-6
EPS_LA = 1.0e-6

ENGINES = ("sync", "vector", "scalar", "tensor", "gpsimd")


class _Sem:
    __slots__ = ("h", "n", "name")


class Prog:

    def __init__(self, nc, ctx):
        self.nc = nc
        self.ctx = ctx
        self.ops = {e: [] for e in ENGINES}
        self.engine_sems = {}
        self.PH = self.sem("PH")

    def sem(self, name):
        s = _Sem()
        s.h = self.ctx.enter_context(self.nc.semaphore(name))
        s.n = 0
        s.name = name
        return s

    def do(self, eng, fn=None, wait=(), inc=(), raw=False):
        es = self.engine_sems.get(eng)
        if es is not None and fn is not None and not raw:
            wait = list(wait) + [(es, es.n)]
            inc = list(inc) + [(es, 1)]
            merged = {}
            for (s, k) in inc:
                merged[id(s)] = (s, merged.get(id(s), (s, 0))[1] + k)
            inc = list(merged.values())
        w = [(s.h, s.n if v is None else v) for (s, v) in wait]
        for (s, k) in inc:
            s.n += k
        i = [(s.h, k) for (s, k) in inc]
        self.ops[eng].append((fn, w, i))

    def barrier(self, dma_sems=()):
        for s in dma_sems:
            q = "gpsimd" if s.name == "ST" else "sync"
            self.do(q, None, wait=[(s, s.n)])
        for e in ENGINES:
            self.do(e, lambda eng: eng.nop(), inc=[(self.PH, 1)], raw=True)
        tgt = self.PH.n
        for e in ENGINES:
            self.do(e, None, wait=[(self.PH, tgt)])

    def replay(self, block):
        def make(en):
            def body(eng):
                for (fn, waits, incs) in self.ops[en]:
                    for (h, v) in waits:
                        eng.wait_ge(h, v)
                    if fn is None:
                        continue
                    instr = fn(eng)
                    for (h, k) in incs:
                        instr.then_inc(h, k)
            return body
        for en in ENGINES:
            getattr(block, en)(make(en))


def fm(dram2d):
    return dram2d.rearrange("(c p) t -> p c t", p=128)


class Builder:
    def __init__(self):
        self.nc = nc = bass.Bass("TRN2", target_bir_lowering=False, debug=False,
                                 detect_race_conditions=True)
        self.ctx = ExitStack()
        self.P = Prog(nc, self.ctx)
        self.LD = self.P.sem("LD")
        self.ST = self.P.sem("ST")
        self.Lp = [self.P.sem(f"Lp{i}") for i in range(4)]
        self.MM = self.P.sem("MM")
        self.EV = self.P.sem("EV")
        self.EA = self.P.sem("EA")
        self.P.engine_sems = {"vector": self.EV, "scalar": self.EA}

        def din(name, shape, dt=F32):
            return nc.dram_tensor(name, shape, dt, kind="ExternalInput").ap()
        self.inp = {}
        for name, shape, dt in [
            ("x9", [D, X9T], F32),
            ("wq_a", [D, D], F32), ("wk_a", [D, D], F32),
            ("wvTa", [D, D], F32), ("woTa", [D, D], F32),
            ("wmq", [MQ, HD], F32), ("wmk", [MQ, HD], F32),
            ("bq_a", [D], F32), ("bk_a", [D], F32),
            ("bv_a", [D], F32), ("bo_a", [D], F32),
            ("Wq_b", [D, D], F32), ("Wk_b", [D, HKV * HD], F32),
            ("Wv_b", [D, HKV * HD], F32), ("Wo_b", [D, D], F32),
            ("wvTf", [D, D], F32), ("wo_f", [D, D], F32),
            ("w_r", [D], F32), ("b_r", [1], F32),
            ("n1w", [D], F32), ("n2w", [D], F32), ("bv_f", [D], F32),
            ("Wg", [D, FF], F32), ("Wu", [D, FF], F32), ("Wd", [FF, D], F32),
            ("maskR", [128, 640], BF16), ("maskF", [4 * 128, 640], BF16),
            ("IDN", [128, 128], BF16), ("IDNF", [128, 128], F32),
            ("onesb", [128, 4], BF16), ("ones1f", [1, 128], F32),
            ("epsv", [1, 2], F32),
        ]:
            self.inp[name] = din(name, shape, dt)

        self.out = nc.dram_tensor("out", [D, OWN], F32,
                                  kind="ExternalOutput").ap()

        dscr = lambda name, shape, dt: nc.dram_tensor(name, shape, dt).ap()
        self.xr = dscr("xr", [D, EXT], F32)
        self.lnbuf = dscr("lnbuf", [D, EXT], BF16)
        self.wqmD = dscr("wqmD", [D, MTOT], BF16)
        self.wkmD = dscr("wkmD", [D, MTOT], BF16)
        self.W2D = dscr("W2D", [D, D], BF16)
        self.WgD = dscr("WgD", [D, FF], BF16)
        self.WuD = dscr("WuD", [D, FF], BF16)
        self.WdD = dscr("WdD", [FF, D], BF16)
        self.woTaD = dscr("woTaD", [D, D], BF16)
        self.WobD = dscr("WobD", [D, D], BF16)
        self.bnc = dscr("bnc", [4096], F32)

        self.taps = {"WobD": (self.WobD, BF16, [D, D]),
                     "xr": (self.xr, F32, [D, EXT]),
                     "lnbuf": (self.lnbuf, BF16, [D, EXT]),
                     "wqmD": (self.wqmD, BF16, [D, MTOT]),
                     "wkmD": (self.wkmD, BF16, [D, MTOT]),
                     "W2D": (self.W2D, BF16, [D, D]),
                     "WgD": (self.WgD, BF16, [D, FF]),
                     "WdD": (self.WdD, BF16, [FF, D]),
                     "woTaD": (self.woTaD, BF16, [D, D])}

        self._names = {}
        self.sbt = lambda name, shape, dt: self.ctx.enter_context(
            nc.sbuf_tensor(name, shape, dt))
        sbt = self.sbt
        self.sIDN = sbt("sIDN", [128, 128], BF16)
        self.sIDNF = sbt("sIDNF", [128, 128], F32)
        self.sOnes = sbt("sOnes", [128, 4], BF16)
        self.sOnes1f = sbt("sOnes1f", [1, 128], F32)
        self.sWrC = sbt("sWrC", [128, NCH], F32)
        self.sBoC = sbt("sBoC", [128, NCH], F32)
        self.sBW2C = sbt("sBW2C", [128, NCH], F32)
        self.sN1C = sbt("sN1C", [128, NCH], F32)
        self.sN2C = sbt("sN2C", [128, NCH], F32)
        self.sBqmC = sbt("sBqmC", [128, 8], F32)
        self.sKsumC = sbt("sKsumC", [128, 8], F32)
        self.sRow = sbt("sRow", [1, 5632], F32)
        self.sTiny = sbt("sTiny", [1, 8], F32)
        self.kv_ctx = ExitStack()

    def SB(self, name, shape, dt):
        k = self._names.get(name, 0)
        self._names[name] = k + 1
        return self.nc.sbuf_tensor(name if k == 0 else f"{name}_{k}",
                                   shape, dt)

    def PS(self, name, shape, dt=F32):
        k = self._names.get(name, 0)
        self._names[name] = k + 1
        return self.nc.psum_tensor(name if k == 0 else f"{name}_{k}",
                                   shape, dt)

    def bar(self):
        self.P.barrier(dma_sems=[self.LD, self.ST] + self.Lp)

    def load(self, dst_ap, src_ap, s=None):
        s = s or self.LD
        self.P.do("sync", None, wait=[(s, s.n)])
        self.P.do("sync", lambda e, d=dst_ap, r=src_ap: e.dma_start(out=d, in_=r),
                  inc=[(s, 16)])
        return (s, s.n)

    def store(self, dst_ap, src_ap, wait=()):
        s = self.ST
        self.P.do("gpsimd", None, wait=[(s, s.n)] + list(wait))
        self.P.do("gpsimd", lambda e, d=dst_ap, r=src_ap: e.dma_start(
            out=d, in_=r), inc=[(s, 16)])
        return (s, s.n)

    def mmv(self, out_ap, lhsT_ap, rhs_ap, start, stop, wait=(), inc=()):
        assert lhsT_ap.base_partition() == rhs_ap.base_partition(), (
            lhsT_ap, rhs_ap)
        self.P.do("tensor", lambda e, o=out_ap, l=lhsT_ap, r=rhs_ap, s=start,
                  t=stop: e.matmul(o, l, r, start=s, stop=t), wait=wait, inc=inc)

    def vec(self, fn, wait=(), inc=()):
        self.P.do("vector", fn, wait=wait, inc=inc)

    def act(self, fn, wait=(), inc=()):
        self.P.do("scalar", fn, wait=wait, inc=inc)

    def colT(self, vec_dram, dst_col, n, t16, pT, extra_wait=()):
        P, LD, MM, EV = self.P, self.LD, self.MM, self.EV
        m = self.load(t16[0:n, 0:128], vec_dram.rearrange("(c p) -> c p", p=128))
        P.do("tensor", lambda e: e.transpose(pT[:, 0:n], t16[0:n, 0:128],
                                             self.sIDNF[0:n, 0:n]),
             wait=[m, (self.EV, self.EV.n)] + list(extra_wait),
             inc=[(MM, 1)])
        self.vec(lambda e: e.tensor_copy(dst_col[:, 0:n], pT[:, 0:n]),
                 wait=[(MM, None)], inc=[(EV, 1)])

    def p0a(self):
        P, LD, ST, MM, EV, EA = self.P, self.LD, self.ST, self.MM, self.EV, self.EA
        nc, I = self.nc, self.inp
        self.load(self.sIDN[:], I["IDN"])
        self.load(self.sIDNF[:], I["IDNF"])
        self.load(self.sOnes[:], I["onesb"])
        self.load(self.sOnes1f[:], I["ones1f"])
        self.load(self.sTiny[0:1, 1:2], I["b_r"][None, :])
        self.load(self.sTiny[0:1, 2:4], I["epsv"])
        m = self.load(self.sRow[0:1, 0:D], I["w_r"][None, :])
        self.vec(lambda e: e.reduce_sum(self.sTiny[0:1, 0:1],
                                        self.sRow[0:1, 0:D], axis=AXX),
                 wait=[m], inc=[(EV, 1)])

        _sc242 = ExitStack()
        t16 = _sc242.enter_context(self.SB('t16', [NCH, 128], F32))
        pT0 = _sc242.enter_context(self.PS('pT0', [128, 64], F32))
        mI = (self.LD, self.LD.n)
        for vec, dst in ((I["w_r"], self.sWrC), (I["bo_a"], self.sBoC),
                         (I["n1w"], self.sN1C), (I["n2w"], self.sN2C)):
            self.colT(vec, dst, NCH, t16, pT0, extra_wait=[mI])
            P.do("sync", None, wait=[(EV, None)])
        _sc242.close()
        self.bar()

        _sc254 = ExitStack()
        wmF = _sc254.enter_context(self.SB('wmF', [MQ, 2, HD], F32))
        wmFb = _sc254.enter_context(self.SB('wmFb', [MQ, 2, HD], BF16))
        wmT = _sc254.enter_context(self.SB('wmT', [128, 2, MQ], BF16))
        bqk = _sc254.enter_context(self.SB('bqk', [NCH, 2, 128], F32))
        bqkb = _sc254.enter_context(self.SB('bqkb', [NCH, 2, 128], BF16))
        bqkT = _sc254.enter_context(self.SB('bqkT', [128, 2 * NCH], BF16))
        wrowq = _sc254.enter_context(self.SB('wrowq', [128, 2, D], F32))
        wrowk = _sc254.enter_context(self.SB('wrowk', [128, 2, D], F32))
        wrowqb = _sc254.enter_context(self.SB('wrowqb', [128, 2, D], BF16))
        wrowkb = _sc254.enter_context(self.SB('wrowkb', [128, 2, D], BF16))
        wqmS = _sc254.enter_context(self.SB('wqmS', [128, NCH, MTOT], BF16))
        wkmS = _sc254.enter_context(self.SB('wkmS', [128, NCH, MTOT], BF16))
        bqmS = _sc254.enter_context(self.SB('bqmS', [NCH, 2 * MQ], F32))
        tb8 = _sc254.enter_context(self.SB('tb8', [8, 128], F32))
        pQm = _sc254.enter_context(self.PS('pQm', [128, MTOT], F32))
        pKm = _sc254.enter_context(self.PS('pKm', [128, MTOT], F32))
        pBm = _sc254.enter_context(self.PS('pBm', [NCH, 2 * MQ], F32))
        pMt = _sc254.enter_context(self.PS('pMt', [128, 2 * MQ + 2 * NCH], BF16))
        pT1 = _sc254.enter_context(self.PS('pT1', [128, 8], F32))
        self.load(wmF[:, 0, :], I["wmq"])
        self.load(wmF[:, 1, :], I["wmk"])
        self.load(bqk[:, 0, :], I["bq_a"].rearrange("(h d) -> h d", d=128))
        m = self.load(bqk[:, 1, :], I["bk_a"].rearrange("(h d) -> h d", d=128))
        self.vec(lambda e: e.tensor_copy(wmFb[:], wmF[:]), wait=[m])
        self.vec(lambda e: e.tensor_copy(bqkb[:], bqk[:]), inc=[(EV, 1)])
        ev0 = EV.n
        P.do("tensor", lambda e: e.transpose(pMt[:, 0:MQ], wmFb[:, 0, :],
                                             self.sIDN[0:MQ, 0:MQ]),
             wait=[(EV, ev0)])
        P.do("tensor", lambda e: e.transpose(pMt[:, MQ:2 * MQ],
                                             wmFb[:, 1, :],
                                             self.sIDN[0:MQ, 0:MQ]))
        P.do("tensor", lambda e: e.transpose(
            pMt[:, 2 * MQ:2 * MQ + NCH], bqkb[:, 0, :],
            self.sIDN[0:NCH, 0:NCH]))
        P.do("tensor", lambda e: e.transpose(
            pMt[:, 2 * MQ + NCH:2 * MQ + 2 * NCH], bqkb[:, 1, :],
            self.sIDN[0:NCH, 0:NCH]), inc=[(MM, 1)])
        mmT = MM.n
        self.vec(lambda e: e.tensor_copy(
            wmT[:].rearrange("p a m -> p (a m)"), pMt[:, 0:2 * MQ]),
            wait=[(MM, mmT)])
        self.vec(lambda e: e.tensor_copy(bqkT[:],
                                         pMt[:, 2 * MQ:2 * MQ + 2 * NCH]),
                 inc=[(EV, 1)])
        evT = EV.n
        self.mmv(pBm[:, 0:MQ], bqkT[:, 0:NCH], wmT[:, 0, :], True, True,
                 wait=[(EV, evT)])
        self.mmv(pBm[:, MQ:2 * MQ], bqkT[:, NCH:2 * NCH], wmT[:, 1, :],
                 True, True, inc=[(MM, 1)])
        self.vec(lambda e: e.tensor_copy(bqmS[:], pBm[:]),
                 wait=[(MM, None)], inc=[(EV, 1)])
        P.do("gpsimd", None, wait=[(ST, ST.n)])
        P.do("gpsimd", lambda e: e.dma_start(
            out=self.bnc[0:MTOT].rearrange("(h m) -> h m", m=MQ),
            in_=bqmS[:, 0:MQ]), wait=[(EV, None)], inc=[(ST, 16)])
        P.do("gpsimd", None, wait=[(ST, ST.n)])
        P.do("gpsimd", lambda e: e.dma_start(
            out=self.bnc[MTOT:2 * MTOT].rearrange("(h m) -> h m", m=MQ),
            in_=bqmS[:, MQ:2 * MQ]), inc=[(ST, 16)])
        P.do("sync", None, wait=[(ST, None)])
        m = self.load(tb8[:], self.bnc[0:MTOT].rearrange("(c p) -> c p",
                                                         p=128))
        P.do("tensor", lambda e: e.transpose(pT1[:], tb8[:],
                                             self.sIDNF[0:8, 0:8]),
             wait=[m], inc=[(MM, 1)])
        self.vec(lambda e: e.tensor_copy(self.sBqmC[:], pT1[:]),
                 wait=[(MM, None)], inc=[(EV, 1)])
        ev_cast = {}
        mm_done = {}
        for hh in range(HQ):
            b = hh % 2
            if hh >= 2:
                P.do("sync", None, wait=[(EV, ev_cast[hh - 2])])
            self.load(wrowq[:, b, :], I["wq_a"][hh * 128:(hh + 1) * 128, :],
                      s=self.Lp[b])
            mload = self.load(wrowk[:, b, :],
                              I["wk_a"][hh * 128:(hh + 1) * 128, :],
                              s=self.Lp[b])
            w = [mload]
            if hh >= 2:
                w.append((MM, mm_done[hh - 2]))
            self.vec(lambda e, b=b: e.tensor_copy(wrowqb[:, b, :],
                                                  wrowq[:, b, :]), wait=w)
            self.vec(lambda e, b=b: e.tensor_copy(wrowkb[:, b, :],
                                                  wrowk[:, b, :]),
                     inc=[(EV, 1)])
            ev_cast[hh] = EV.n
            w0 = [(EV, ev_cast[hh])]
            if hh >= 1:
                w0.append((EV, ev_cast.get(("evac", hh - 1), 0)))
            for dc in range(NCH):
                self.mmv(pQm[:, dc * MQ:(dc + 1) * MQ],
                         wrowqb[:, b, dc * 128:(dc + 1) * 128],
                         wmT[:, 0, :], True, True,
                         wait=w0 if dc == 0 else ())
                self.mmv(pKm[:, dc * MQ:(dc + 1) * MQ],
                         wrowkb[:, b, dc * 128:(dc + 1) * 128],
                         wmT[:, 1, :], True, True,
                         inc=[(MM, 1)] if dc == NCH - 1 else ())
            mm_done[hh] = MM.n
            self.vec(lambda e, hh=hh: e.tensor_copy(
                wqmS[:, :, hh * MQ:(hh + 1) * MQ],
                pQm[:].rearrange("p (c m) -> p c m", m=MQ)),
                wait=[(MM, mm_done[hh])])
            self.vec(lambda e, hh=hh: e.tensor_copy(
                wkmS[:, :, hh * MQ:(hh + 1) * MQ],
                pKm[:].rearrange("p (c m) -> p c m", m=MQ)),
                inc=[(EV, 1)])
            ev_cast[("evac", hh)] = EV.n
        P.do("gpsimd", None, wait=[(ST, ST.n)])
        P.do("gpsimd", lambda e: e.dma_start(out=fm(self.wqmD), in_=wqmS[:]),
             wait=[(EV, None)], inc=[(ST, 16)])
        P.do("gpsimd", None, wait=[(ST, ST.n)])
        P.do("gpsimd", lambda e: e.dma_start(out=fm(self.wkmD), in_=wkmS[:]),
             inc=[(ST, 16)])
        P.do("gpsimd", None, wait=[(ST, None)])
        _sc254.close()
        self.bar()

    def p0b(self):
        P, LD, ST, MM, EV, EA = self.P, self.LD, self.ST, self.MM, self.EV, self.EA
        nc, I = self.nc, self.inp
        _sc378 = ExitStack()
        wvTb = _sc378.enter_context(self.SB('wvTb', [128, NCH, D], BF16))
        wob = _sc378.enter_context(self.SB('wob', [128, NCH, D], BF16))
        wldA = _sc378.enter_context(self.SB('wldA', [128, 2, D], F32))
        bvT = _sc378.enter_context(self.SB('bvT', [128, NCH], F32))
        bvTb = _sc378.enter_context(self.SB('bvTb', [128, NCH], BF16))
        t16 = _sc378.enter_context(self.SB('t16', [NCH, 128], F32))
        w2st = _sc378.enter_context(self.SB('w2st', [128, 2, D], BF16))
        pW = _sc378.enter_context(self.PS('pW', [128, 512], F32))
        pW2 = _sc378.enter_context(self.PS('pW2', [128, 512], F32))
        pT0 = _sc378.enter_context(self.PS('pT0', [128, 64], F32))
        pRw = _sc378.enter_context(self.PS('pRw', [1, 2048], F32))
        self.colT(I["bv_f"], bvT, NCH, t16, pT0)
        self.vec(lambda e: e.tensor_copy(bvTb[:], bvT[:]), wait=[(EV, None)],
                 inc=[(EV, 1)])
        ev_cast = {}
        for i in range(2 * NCH):
            mat, dstb = (I["wvTf"], wvTb) if i % 2 == 0 else (I["wo_f"], wob)
            cc = i // 2
            b = i % 2
            if i >= 2:
                P.do("sync", None, wait=[(EV, ev_cast[i - 2])])
            m = self.load(wldA[:, b, :], mat[cc * 128:(cc + 1) * 128, :],
                          s=self.Lp[b])
            self.vec(lambda e, b=b, dstb=dstb, cc=cc: e.tensor_copy(
                dstb[:, cc, :], wldA[:, b, :]), wait=[m],
                inc=[(EV, 1)])
            ev_cast[i] = EV.n
        evAll = EV.n
        ev_stage = {}
        for dc in range(NCH):
            b = dc % 2
            if dc >= 2:
                pass
        mmN = 0
        stw = {}
        for dc in range(NCH):
            b = dc % 2
            for eg in range(4):
                pp = pW if eg % 2 == 0 else pW2
                for cc in range(NCH):
                    w = [(EV, evAll)] if (dc == 0 and eg == 0 and cc == 0) else ()
                    if cc == 0 and (dc * 4 + eg) >= 2:
                        w = [(EV, ev_stage[dc * 4 + eg - 2])]
                    self.mmv(pp[:, 0:512],
                             wvTb[:, cc, dc * 128:(dc + 1) * 128],
                             wob[:, cc, eg * 512:(eg + 1) * 512],
                             cc == 0, cc == NCH - 1, wait=w,
                             inc=[(MM, 1)] if cc == NCH - 1 else ())
                mmN = self.MM.n
                wv2 = [(MM, mmN)]
                if eg == 0 and dc >= 2:
                    wv2.append((ST, stw[dc - 2]))
                self.vec(lambda e, b=b, dc=dc, eg=eg, pp=pp: e.tensor_scalar(
                    out=w2st[:, b, eg * 512:(eg + 1) * 512], in0=pp[:, 0:512],
                    scalar1=self.sN1C[:, dc:dc + 1], scalar2=None,
                    op0=ALU.mult), wait=wv2, inc=[(EV, 1)])
                ev_stage[dc * 4 + eg] = EV.n
            P.do("gpsimd", None, wait=[(ST, ST.n)])
            P.do("gpsimd", lambda e, b=b, dc=dc: e.dma_start(
                out=fm(self.W2D)[:, dc, :], in_=w2st[:, b, :]),
                wait=[(EV, ev_stage[dc * 4 + 3])], inc=[(ST, 16)])
            stw[dc] = ST.n
        for eg in range(4):
            for cc in range(NCH):
                self.mmv(pRw[0:1, eg * 512:(eg + 1) * 512],
                         bvTb[:, cc:cc + 1], wob[:, cc, eg * 512:(eg + 1) * 512],
                         cc == 0, cc == NCH - 1,
                         inc=[(MM, 1)] if cc == NCH - 1 else ())
        self.vec(lambda e: e.tensor_copy(self.sRow[0:1, 0:2048], pRw[:]),
                 wait=[(MM, None)], inc=[(EV, 1)])
        P.do("gpsimd", None, wait=[(ST, ST.n)])
        P.do("gpsimd", None, wait=[(ST, ST.n)])
        P.do("gpsimd", lambda e: e.dma_start(
            out=self.bnc[2048:2048 + D][None, :],
            in_=self.sRow[0:1, 0:2048]),
             wait=[(EV, None)], inc=[(ST, 16)])
        P.do("sync", None, wait=[(ST, None)])
        self.colT(self.bnc[2048:2048 + D], self.sBW2C, NCH, t16, pT0)
        _sc378.close()
        self.bar()

    def p0w(self):
        P, ST, EV = self.P, self.ST, self.EV
        I = self.inp
        sc = ExitStack()
        cw = sc.enter_context(self.SB("w0cw", [128, 2, D], F32))
        cwb = sc.enter_context(self.SB("w0cwb", [128, 2, D], BF16))
        stw = {}
        for rr in range(NCH):
            b = rr % 2
            if rr >= 2:
                P.do("sync", None, wait=[(ST, stw[rr - 2])])
            m = self.load(cw[:, b, :], I["woTa"][rr * 128:(rr + 1) * 128, :],
                          s=self.Lp[b])
            self.vec(lambda e, b=b: e.tensor_copy(cwb[:, b, :], cw[:, b, :]),
                     wait=[m], inc=[(EV, 1)])
            mk = self.EV.n
            P.do("gpsimd", None, wait=[(ST, ST.n)])
            P.do("gpsimd", lambda e, b=b, rr=rr: e.dma_start(
                out=self.woTaD[rr * 128:(rr + 1) * 128, :],
                in_=cwb[:, b, :]), wait=[(EV, mk)], inc=[(ST, 16)])
            stw[rr] = self.ST.n
        P.do("gpsimd", None, wait=[(ST, None)])
        sc.close()
        self.bar()

    def p0c_prepare(self):
        I = self.inp
        CP = FF // 4
        self.p0c_ctx = ExitStack()
        self.p0c_cw = self.p0c_ctx.enter_context(
            self.SB("p0cw", [128, 2, CP], F32))
        self.p0c_cwb = self.p0c_ctx.enter_context(
            self.SB("p0cwb", [128, 2, CP], BF16))
        self.p0c_queue = []
        for mat, dst, nrow, ncol, scale in [
                (I["Wg"], self.WgD, NCH, FF, True),
                (I["Wu"], self.WuD, NCH, FF, True),
                (I["Wd"], self.WdD, FF // 128, D, False),
                (I["Wo_b"], self.WobD, NCH, D, False)]:
            nh = (ncol + CP - 1) // CP
            for rr in range(nrow):
                for hf in range(nh):
                    c0 = hf * CP
                    cl = min(CP, ncol - c0)
                    self.p0c_queue.append((mat, dst, rr, c0, cl, scale))
        self.p0c_i = 0
        self.p0c_ev = {}
        self.p0c_st = {}

    def p0c_step(self, n):
        P, LD, ST, MM, EV, EA = self.P, self.LD, self.ST, self.MM, self.EV, self.EA
        cw, cwb = self.p0c_cw, self.p0c_cwb
        for _ in range(n):
            if self.p0c_i >= len(self.p0c_queue):
                return
            i = self.p0c_i
            mat, dst, rr, c0, cl, scale = self.p0c_queue[i]
            b = i % 2
            if i >= 2:
                P.do("sync", None, wait=[(ST, self.p0c_st[i - 2])])
            m = self.load(cw[:, b, 0:cl],
                          mat[rr * 128:(rr + 1) * 128, c0:c0 + cl])
            if scale:
                self.vec(lambda e, b=b, rr=rr, cl=cl: e.tensor_scalar(
                    out=cwb[:, b, 0:cl], in0=cw[:, b, 0:cl],
                    scalar1=self.sN2C[:, rr:rr + 1], scalar2=None,
                    op0=ALU.mult), wait=[m], inc=[(EV, 1)])
                self.p0c_ev[i] = ("EV", self.EV.n)
            else:
                self.act(lambda e, b=b, cl=cl: e.activation(
                    cwb[:, b, 0:cl], cw[:, b, 0:cl], AF.Copy),
                    wait=[m], inc=[(EA, 1)])
                self.p0c_ev[i] = ("EA", self.EA.n)
            kind, mk = self.p0c_ev[i]
            sem = self.EV if kind == "EV" else self.EA
            P.do("gpsimd", None, wait=[(ST, ST.n)])
            P.do("gpsimd", lambda e, b=b, rr=rr, dst=dst, c0=c0, cl=cl:
                 e.dma_start(out=dst[rr * 128:(rr + 1) * 128, c0:c0 + cl],
                             in_=cwb[:, b, 0:cl]),
                 wait=[(sem, mk)], inc=[(ST, 16)])
            self.p0c_st[i] = self.ST.n
            self.p0c_i += 1

    def p0c_flush(self):
        self.p0c_step(10 ** 9)
        self.P.do("gpsimd", None, wait=[(self.ST, None)])
        self.p0c_ctx.close()

    R_MEAN, R_S1, R_S2, R_VAR, R_SD, R_RSTD, R_NEGMR = 0, 512, 1024, 1536, 2048, 2560, 3072
    R_RSTD1, R_RS, R_M01, R_S3 = 3584, 4096, 4608, 5120

    def emit_ln(self, xq, xsq, lnb, pSum, pSsq, pRt, pBs, bcQ, scr512=None,
                tmpf32=None, want_ffn=False, T=512):
        P, MM, EV, EA = self.P, self.MM, self.EV, self.EA
        R = self.sRow
        self.vec(lambda e: e.tensor_copy(
            lnb[:].rearrange("p c t -> p (c t)"),
            xq[:].rearrange("p c t -> p (c t)")), inc=[(EV, 1)])
        evXB = self.EV.n
        self.act(lambda e: e.activation(
            xsq[:].rearrange("p c t -> p (c t)"),
            xq[:].rearrange("p c t -> p (c t)"), AF.Square), inc=[(EA, 1)])
        eaSQ = self.EA.n
        for dc in range(NCH):
            self.mmv(pSum[0:1, 0:T], self.sOnes[:, 0:1], lnb[:, dc, :],
                     dc == 0, dc == NCH - 1, wait=[(EV, evXB)] if dc == 0 else ())
            self.mmv(pSsq[0:1, 0:T], self.sOnes[:, 0:1], xsq[:, dc, :],
                     dc == 0, dc == NCH - 1,
                     wait=[(EA, eaSQ)] if dc == 0 else (),
                     inc=[(MM, 1)] if (dc == NCH - 1 and not want_ffn) else ())
            if want_ffn:
                self.mmv(pRt[0:1, 0:T], self.sWrC[:, dc:dc + 1], xq[:, dc, :],
                         dc == 0, dc == NCH - 1,
                         inc=[(MM, 1)] if dc == NCH - 1 else ())
        mmStats = self.MM.n
        RM, RS1, RS2, RV = self.R_MEAN, self.R_S1, self.R_S2, self.R_VAR
        RSD, RR, RNM, RR1 = self.R_SD, self.R_RSTD, self.R_NEGMR, self.R_RSTD1
        RRS, RM01, RS3 = self.R_RS, self.R_M01, self.R_S3
        r = lambda o: R[0:1, o:o + T]
        self.vec(lambda e: e.tensor_scalar(out=r(RM), in0=pSum[0:1, 0:T],
                                           scalar1=1.0 / D, scalar2=None,
                                           op0=ALU.mult), wait=[(MM, mmStats)])
        self.vec(lambda e: e.tensor_scalar(out=r(RS1), in0=pSsq[0:1, 0:T],
                                           scalar1=1.0 / D, scalar2=None,
                                           op0=ALU.mult))
        self.vec(lambda e: e.tensor_tensor(out=r(RS2), in0=r(RM), in1=r(RM),
                                           op=ALU.mult))
        self.vec(lambda e: e.tensor_tensor(out=r(RV), in0=r(RS1), in1=r(RS2),
                                           op=ALU.subtract), inc=[(EV, 1)])
        evV = self.EV.n
        self.act(lambda e: e.activation(r(RSD), r(RV), AF.Sqrt, bias=self.sTiny[0:1, 2:3]),
                 wait=[(EV, evV)], inc=[(EA, 1)])
        eaSD = self.EA.n
        self.vec(lambda e: e.reciprocal(r(RR), r(RSD)), wait=[(EA, eaSD)])
        self.vec(lambda e: e.scalar_tensor_tensor(
            out=r(RNM), in0=r(RM), scalar=-1.0, in1=r(RR), op0=ALU.mult,
            op1=ALU.mult), inc=[(EV, 1)])
        if want_ffn:
            self.vec(lambda e: e.tensor_tensor(out=r(RS3), in0=r(RV), in1=r(RR),
                                               op=ALU.mult))
            self.vec(lambda e: e.tensor_tensor(out=r(RS3), in0=r(RS3), in1=r(RR),
                                               op=ALU.mult), inc=[(EV, 1)])
            self.act(lambda e: e.activation(r(RS2), r(RS3), AF.Sqrt,
                                            bias=self.sTiny[0:1, 3:4]),
                     wait=[(EV, None)], inc=[(EA, 1)])
            self.vec(lambda e: e.reciprocal(r(RR1), r(RS2)), wait=[(EA, None)])
            self.vec(lambda e: e.tensor_scalar(
                out=r(RS3), in0=r(RM), scalar1=self.sTiny[0:1, 0:1],
                scalar2=None, op0=ALU.mult))
            self.vec(lambda e: e.tensor_tensor(out=r(RS3), in0=pRt[0:1, 0:T],
                                               in1=r(RS3), op=ALU.subtract))
            self.vec(lambda e: e.tensor_tensor(out=r(RS3), in0=r(RS3),
                                               in1=r(RR), op=ALU.mult))
            self.vec(lambda e: e.tensor_scalar(
                out=r(RRS), in0=r(RS3), scalar1=self.sTiny[0:1, 1:2],
                scalar2=None, op0=ALU.add))
            self.vec(lambda e: e.tensor_single_scalar(r(RM01), r(RRS), 0.0,
                                                      op=ALU.is_gt),
                     inc=[(EV, 1)])
        evRows = self.EV.n
        rows = [RR, RNM, RR1, RM01] if want_ffn else [RR, RNM]
        for i, ro in enumerate(rows):
            self.mmv(pBs[i][:, 0:T], self.sOnes1f[:], R[0:1, ro:ro + T],
                     True, True, wait=[(EV, evRows)] if i == 0 else (),
                     inc=[(MM, 1)] if i == len(rows) - 1 else ())
        mmB = self.MM.n
        for i in range(len(rows)):
            self.vec(lambda e, i=i: e.tensor_copy(bcQ[:, i * 512:i * 512 + T],
                                                  pBs[i][:, 0:T]),
                     wait=[(MM, mmB)] if i == 0 else (),
                     inc=[(EV, 1)] if i == len(rows) - 1 else ())
        evBC = self.EV.n
        bcR = bcQ[:, 0:T]
        bcN = bcQ[:, 512:512 + T]
        if tmpf32 is not None:
            bc3 = lambda c: c[:, None, :].to_broadcast((128, NCH, T))
            self.vec(lambda e: e.tensor_tensor(out=tmpf32[:], in0=xq[:],
                                               in1=bc3(bcR), op=ALU.mult),
                     wait=[(EV, evBC)])
            self.vec(lambda e: e.tensor_tensor(out=lnb[:], in0=tmpf32[:],
                                               in1=bc3(bcN), op=ALU.add),
                     inc=[(EV, 1)])
        else:
            for dc in range(NCH):
                self.vec(lambda e, dc=dc: e.tensor_tensor(
                    out=scr512[:, 0:T], in0=xq[:, dc, :], in1=bcR,
                    op=ALU.mult), wait=[(EV, evBC)] if dc == 0 else ())
                self.vec(lambda e, dc=dc: e.tensor_tensor(
                    out=lnb[:, dc, :], in0=scr512[:, 0:T], in1=bcN,
                    op=ALU.add), inc=[(EV, 1)] if dc == NCH - 1 else ())
        return self.EV.n

    def s1a(self):
        P, LD, ST, MM, EV, EA = self.P, self.LD, self.ST, self.MM, self.EV, self.EA
        nc, I = self.nc, self.inp
        TT = 256
        NT = X9T // TT
        self.sKvB = self.kv_ctx.enter_context(
            self.SB("sKvB", [128, HQ * HD], BF16))
        sc = ExitStack()
        ent = sc.enter_context
        sBkmB = ent(self.SB("sBkmB", [128, MTOT], F32))
        sBvaB = ent(self.SB("sBvaB", [128, D], F32))
        wkmSB = ent(self.SB("wkmSB", [128, NCH, MTOT], BF16))
        wvab = ent(self.SB("wvab", [128, NCH, D], BF16))
        wvld = ent(self.SB("wvld", [128, 2, D], F32))
        xt1 = ent(self.SB("xt1", [128, NCH, TT], F32))
        lnb1 = ent(self.SB("lnb1", [128, NCH, TT], BF16))
        xsq1 = ent(self.SB("xsq1", [128, NCH, TT], BF16))
        scrA = ent(self.SB("scrA", [128, 256], F32))
        zf = ent(self.SB("zf", [128, MTOT], BF16))
        mf = ent(self.SB("mf", [128, MTOT], BF16))
        ef = ent(self.SB("ef", [128, MTOT], BF16))
        kpb = ent(self.SB("kpb", [128, MTOT], BF16))
        vsb = ent(self.SB("vsb", [128, D], BF16))
        kvAcc = ent(self.SB("kvAcc", [64, HQ * HD], F32))
        ksAcc = ent(self.SB("ksAcc", [128, 8], F32))
        bcQ1 = ent(self.SB("bcQ1", [128, 768], F32))
        kvP = ent(self.PS("kvP", [64, HQ * HD], F32))
        pKs = ent(self.PS("pKs", [128, 8], F32))
        T4 = ent(self.PS("T4", [128, 1024], F32))
        T2a = ent(self.PS("T2a", [128, 512], F32))
        self.load(sBkmB[:],
                  self.bnc[MTOT:2 * MTOT][None, :].to_broadcast((128, MTOT)))
        self.load(sBvaB[:], I["bv_a"][None, :].to_broadcast((128, D)))
        self.load(wkmSB[:], fm(self.wkmD))
        ev_c = {}
        for cc in range(NCH):
            b = cc % 2
            if cc >= 2:
                P.do("sync", None, wait=[(EV, ev_c[cc - 2])])
            m = self.load(wvld[:, b, :], I["wvTa"][cc * 128:(cc + 1) * 128, :],
                          s=self.Lp[b])
            self.vec(lambda e, b=b, cc=cc: e.tensor_copy(
                wvab[:, cc, :], wvld[:, b, :]), wait=[m],
                inc=[(EV, 1)])
            ev_c[cc] = EV.n
        self.bar()

        kv_started = False
        kvEvPrev = 0
        mmPrev = 0
        evPrev = 0
        stLn = 0
        vEA = 0
        for t in range(NT):
            P.do("sync", None, wait=[(EV, evPrev), (EV, vEA), (ST, stLn)])
            mx = self.load(xt1[:], fm(I["x9"])[:, :, t * TT:(t + 1) * TT],
                           s=self.Lp[2])
            P.do("vector", None, wait=[(MM, mmPrev), (ST, stLn), mx])
            P.do("scalar", None, wait=[(MM, mmPrev), mx])
            P.do("tensor", None, wait=[(EV, vEA), (EV, evPrev)])
            evLn = self.emit_ln(xt1, xsq1, lnb1, T2a, T4, None,
                                [T4[:, 0:512], T4[:, 512:1024]], bcQ1,
                                scr512=scrA, T=TT)
            if t < EXT // TT:
                self.store(fm(self.lnbuf)[:, :, t * TT:(t + 1) * TT], lnb1[:],
                           wait=[(EV, evLn)])
                stLn = self.ST.n
            if t < 2:
                mmPrev = self.MM.n
                evPrev = evLn
                continue
            last_t = t == NT - 1
            for s in range(2):
                for dc in range(NCH):
                    w = ()
                    if dc == 0:
                        w = [(EV, evLn), (EV, vEA)]
                    self.mmv(T4[:, 0:512], lnb1[:, dc, s * 128:(s + 1) * 128],
                             wkmSB[:, dc, 0:512], dc == 0, dc == NCH - 1,
                             wait=w)
                    self.mmv(T4[:, 512:1024],
                             lnb1[:, dc, s * 128:(s + 1) * 128],
                             wkmSB[:, dc, 512:1024], dc == 0, dc == NCH - 1,
                             inc=[(MM, 1)] if dc == NCH - 1 else ())
                mmK = self.MM.n
                self.vec(lambda e: e.tensor_tensor(out=zf[:], in0=T4[:],
                                                   in1=sBkmB[:], op=ALU.add),
                         wait=[(MM, mmK), (EA, None)])
                self.vec(lambda e: e.tensor_scalar(
                    out=mf[:], in0=zf[:], scalar1=0.0, scalar2=None,
                    op0=ALU.min), inc=[(EV, 1)])
                self.act(lambda e: e.activation(ef[:], mf[:], AF.Exp),
                         wait=[(EV, None)], inc=[(EA, 1)])
                self.vec(lambda e: e.scalar_tensor_tensor(
                    out=kpb[:], in0=zf[:], scalar=0.0, in1=ef[:],
                    op0=ALU.max, op1=ALU.add), wait=[(EA, None)],
                    inc=[(EV, 1)])
                evK = self.EV.n
                first = not kv_started
                for c in range(8):
                    self.mmv(pKs[:, c:c + 1], kpb[:, c * 128:(c + 1) * 128],
                             self.sOnes[:, 0:1], True, True,
                             wait=[(EV, evK), (EV, kvEvPrev)] if c == 0 else ())
                vbanks = [T2a[:, 0:512], T4[:, 0:512], T4[:, 512:1024],
                          T2a[:, 0:512]]
                eaV = {}
                for eg in range(4):
                    pb = vbanks[eg]
                    for dc in range(NCH):
                        w = ()
                        if dc == 0:
                            if eg in (1, 2):
                                w = [(EV, evK)]
                            elif eg == 3:
                                w = [(EV, eaV[0])]
                        self.mmv(pb, lnb1[:, dc, s * 128:(s + 1) * 128],
                                 wvab[:, dc, eg * 512:(eg + 1) * 512],
                                 dc == 0, dc == NCH - 1, wait=w,
                                 inc=[(MM, 1)] if dc == NCH - 1 else ())
                    mmV = self.MM.n
                    self.vec(lambda e, eg=eg, pb=pb: e.tensor_tensor(
                        out=vsb[:, eg * 512:(eg + 1) * 512], in0=pb,
                        in1=sBvaB[:, eg * 512:(eg + 1) * 512], op=ALU.add),
                        wait=[(MM, mmV)], inc=[(EV, 1)])
                    eaV[eg] = self.EV.n
                vEA = self.EV.n
                for hh in range(HQ):
                    self.mmv(kvP[0:64, hh * 128:(hh + 1) * 128],
                             kpb[:, hh * MQ:(hh + 1) * MQ],
                             vsb[:, hh * 128:(hh + 1) * 128],
                             True, True,
                             wait=[(EV, vEA), (EV, kvEvPrev)] if hh == 0 else (),
                             inc=[(MM, 1)] if hh == HQ - 1 else ())
                mmKV = self.MM.n
                if first:
                    self.vec(lambda e: e.tensor_copy(kvAcc[:], kvP[:]),
                             wait=[(MM, mmKV)])
                    self.vec(lambda e: e.tensor_copy(ksAcc[:], pKs[:]),
                             inc=[(EV, 1)])
                else:
                    self.vec(lambda e: e.tensor_tensor(
                        out=kvAcc[:], in0=kvP[:], in1=kvAcc[:], op=ALU.add),
                        wait=[(MM, mmKV)])
                    self.vec(lambda e: e.tensor_tensor(
                        out=ksAcc[:], in0=pKs[:], in1=ksAcc[:], op=ALU.add),
                        inc=[(EV, 1)])
                kvEvPrev = self.EV.n
                kv_started = True
            mmPrev = self.MM.n
            evPrev = self.EV.n
        self.vec(lambda e: e.tensor_copy(self.sKvB[0:64, :], kvAcc[:]),
                 wait=[(EV, None)])
        self.vec(lambda e: e.tensor_copy(self.sKvB[64:128, :], kvAcc[:]))
        self.vec(lambda e: e.tensor_copy(self.sKsumC[:], ksAcc[:]),
                 inc=[(EV, 1)])
        sc.close()
        self.bar()

    def s1b(self):
        P, LD, ST, MM, EV, EA = self.P, self.LD, self.ST, self.MM, self.EV, self.EA
        nc, I = self.nc, self.inp
        _sc762 = ExitStack()
        wqmSB = _sc762.enter_context(self.SB('wqmSB', [128, NCH, MTOT], BF16))
        lnq = _sc762.enter_context(self.SB('lnq', [128, NCH, 512], BF16))
        xq2 = _sc762.enter_context(self.SB('xq2', [128, NCH, 512], F32))
        qsb = _sc762.enter_context(self.SB('qsb', [128, 8, 512], BF16))
        qkb = _sc762.enter_context(self.SB('qkb', [128, 512], BF16))
        zf = _sc762.enter_context(self.SB('zf', [128, 512], F32))
        mf = _sc762.enter_context(self.SB('mf', [128, 512], F32))
        ef = _sc762.enter_context(self.SB('ef', [128, 512], F32))
        recb = _sc762.enter_context(self.SB('recb', [128, 512], F32))
        attnO = _sc762.enter_context(self.SB('attnO', [128, NCH, 512], BF16))
        wol = _sc762.enter_context(self.SB('wol', [128, 4, 1024], BF16))
        PD0 = _sc762.enter_context(self.PS('PD0', [128, 512], F32))
        PD1 = _sc762.enter_context(self.PS('PD1', [128, 512], F32))
        PD2 = _sc762.enter_context(self.PS('PD2', [128, 512], F32))
        PD3 = _sc762.enter_context(self.PS('PD3', [128, 512], F32))
        PD4 = _sc762.enter_context(self.PS('PD4', [128, 512], F32))
        PD5 = _sc762.enter_context(self.PS('PD5', [128, 512], F32))
        PD6 = _sc762.enter_context(self.PS('PD6', [128, 512], F32))
        PD7 = _sc762.enter_context(self.PS('PD7', [128, 512], F32))
        PDs = [PD0, PD1, PD2, PD3, PD4, PD5, PD6, PD7]
        sRowB = _sc762.enter_context(self.SB("sRowB", [1, 8192], F32))
        self.load(wqmSB[:], fm(self.wqmD))
        self.p0c_prepare()
        self.bar()
        evO = 0
        for t in range(5):
            self.p0c_step(56)
            P.do("sync", None, wait=[(EV, None), (ST, None)])
            mln = self.load(lnq[:], fm(self.lnbuf)[:, :, t * 512:(t + 1) * 512], s=self.Lp[2])
            mxx = self.load(xq2[:], fm(I["x9"])[:, :, t * 512:(t + 1) * 512], s=self.Lp[3])
            evQ = {}
            for mc in range(8):
                pq = PDs[mc % 2]
                for dc in range(NCH):
                    w = ()
                    if dc == 0:
                        w = [mln]
                        if mc >= 2:
                            w.append((EV, evQ[mc - 2]))
                        elif t > 0:
                            w.append((EV, evO))
                    self.mmv(pq[:, 0:512], wqmSB[:, dc, mc * 128:(mc + 1) * 128],
                             lnq[:, dc, :], dc == 0, dc == NCH - 1, wait=w,
                             inc=[(MM, 1)] if dc == NCH - 1 else ())
                mmQ = self.MM.n
                self.vec(lambda e, pq=pq, mc=mc: e.tensor_scalar(
                    out=zf[:], in0=pq[:, 0:512],
                    scalar1=self.sBqmC[:, mc:mc + 1], scalar2=None,
                    op0=ALU.add), wait=[(MM, mmQ), (EA, None)])
                self.vec(lambda e: e.tensor_scalar(
                    out=mf[:], in0=zf[:], scalar1=0.0, scalar2=None,
                    op0=ALU.min), inc=[(EV, 1)])
                self.act(lambda e: e.activation(ef[:], mf[:], AF.Exp),
                         wait=[(EV, None)], inc=[(EA, 1)])
                self.vec(lambda e, mc=mc: e.scalar_tensor_tensor(
                    out=qsb[:, mc, :], in0=zf[:], scalar=0.0, in1=ef[:],
                    op0=ALU.max, op1=ALU.add), wait=[(EA, None)],
                    inc=[(EV, 1)])
                evQ[mc] = self.EV.n
            evN = {}
            for c in range(8):
                self.vec(lambda e, c=c: e.tensor_scalar(
                    out=qkb[:], in0=qsb[:, c, :],
                    scalar1=self.sKsumC[:, c:c + 1], scalar2=None,
                    op0=ALU.mult), wait=[(EV, evQ[c])] +
                    ([(MM, None)] if c > 0 else []), inc=[(EV, 1)])
                evqk = self.EV.n
                self.mmv(PD2[0:1, 0:512], self.sOnes[:, 1:2], qkb[:],
                         True, True, wait=[(EV, evqk)])
                self.mmv(PD3[0:1, 0:512], self.sOnes[:, 2:3], qkb[:],
                         True, True, inc=[(MM, 1)])
                mmN = self.MM.n
                self.vec(lambda e, c=c: e.tensor_copy(
                    sRowB[0:1, c * 1024:c * 1024 + 512],
                    PD2[0:1, 0:512]), wait=[(MM, mmN)])
                self.vec(lambda e, c=c: e.tensor_copy(
                    sRowB[0:1, c * 1024 + 512:(c + 1) * 1024],
                    PD3[0:1, 0:512]), inc=[(EV, 1)])
                evN[c] = self.EV.n
            self.vec(lambda e: e.tensor_scalar(
                out=sRowB[0:1, 0:8192], in0=sRowB[0:1, 0:8192],
                scalar1=EPS_LA, scalar2=None, op0=ALU.add),
                wait=[(EV, evN[7])])
            self.vec(lambda e: e.reciprocal(sRowB[0:1, 0:8192],
                                            sRowB[0:1, 0:8192]),
                     inc=[(EV, 1)])
            evR = self.EV.n
            for hh in range(HQ):
                off = (hh // 2) * 1024 + (hh % 2) * 512
                self.mmv(PD4[:, 0:512], self.sOnes1f[:],
                         sRowB[0:1, off:off + 512], True, True,
                         wait=[(EV, evR)] if hh == 0 else [(EV, None)],
                         inc=[(MM, 1)])
                mmB = self.MM.n
                self.act(lambda e: e.activation(recb[:], PD4[:, 0:512],
                                                AF.Copy),
                         wait=[(MM, mmB), (EV, None)], inc=[(EA, 1)])
                eaB = self.EA.n
                base = (hh % 2) * 64
                self.mmv(PD5[:, 0:512],
                         self.sKvB[base:base + 64, hh * 128:(hh + 1) * 128],
                         qsb[base:base + 64, hh // 2, :],
                         True, True, wait=[(EV, None)], inc=[(MM, 1)])
                mmU = self.MM.n
                self.vec(lambda e, hh=hh: e.tensor_tensor(
                    out=attnO[:, hh, :], in0=PD5[:, 0:512], in1=recb[:],
                    op=ALU.mult), wait=[(MM, mmU), (EA, eaB)],
                    inc=[(EV, 1)])
            evAO = self.EV.n
            P.do("sync", None, wait=[mxx])
            for eh in range(2):
                ev_wol = {}
                for cp in range(NCH // 2):
                    g = cp % 2
                    if cp >= 2:
                        P.do("sync", None, wait=[(MM, ev_wol[cp - 2])])
                    else:
                        P.do("sync", None, wait=[(MM, MM.n)])
                    wv3 = wol[:, g * 2:g * 2 + 2, :]
                    mwo = self.load(
                        wv3,
                        self.woTaD[cp * 256:(cp + 1) * 256,
                                   eh * 1024:(eh + 1) * 1024]
                        .rearrange("(a p) e -> p a e", p=128),
                        s=self.Lp[g])
                    for fi in range(2):
                        cc = cp * 2 + fi
                        for e8 in range(8):
                            w = [mwo]
                            if cc == 0:
                                w = [mwo, (EV, None)]
                            self.mmv(PDs[e8][:, 0:512],
                                     wv3[:, fi, e8 * 128:(e8 + 1) * 128],
                                     attnO[:, cc, :], cc == 0, cc == NCH - 1,
                                     wait=w if (e8 == 0 or cc == 0) else (),
                                     inc=[(MM, 1)] if e8 == 7 else ())
                    ev_wol[cp] = self.MM.n
                mmO = self.MM.n
                for e8 in range(8):
                    ec = eh * 8 + e8
                    self.vec(lambda e, e8=e8, ec=ec: e.scalar_tensor_tensor(
                        out=xq2[:, ec, :], in0=PDs[e8][:, 0:512],
                        scalar=self.sBoC[:, ec:ec + 1],
                        in1=xq2[:, ec, :], op0=ALU.add, op1=ALU.add),
                        wait=[(MM, mmO)] if e8 == 0 else (),
                        inc=[(EV, 1)] if e8 == 7 else ())
                evO = self.EV.n
            self.store(fm(self.xr)[:, :, t * 512:(t + 1) * 512], xq2[:],
                       wait=[(EV, evO)])
            P.do("sync", None, wait=[(ST, None)])
        self.p0c_flush()
        _sc762.close()
        self.kv_ctx.close()
        self.bar()

    def ffn_quarter(self, src3d, dst3d, tiles, first_pass):
        P, LD, ST, MM, EV, EA = self.P, self.LD, self.ST, self.MM, self.EV, self.EA
        nc = self.nc
        NFC = FF // 128
        _sc915 = ExitStack()
        fxq = _sc915.enter_context(self.SB('fxq', [128, NCH, 512], F32))
        fh = _sc915.enter_context(self.SB('fh', [128, NCH, 512], BF16))
        fln = _sc915.enter_context(self.SB('fln', [128, NCH, 512], BF16))
        fR5 = _sc915.enter_context(self.SB('fR5', [128, NCH, 512], BF16))
        fR6 = _sc915.enter_context(self.SB('fR6', [128, NCH, 512], BF16))
        fG = _sc915.enter_context(self.SB('fG', [128, NFC, 512], BF16))
        fbcQ = _sc915.enter_context(self.SB('fbcQ', [128, 2560], F32))
        fscr = _sc915.enter_context(self.SB('fscr', [128, 512], F32))
        fgl = _sc915.enter_context(self.SB('fgl', [128, 512], BF16))
        wsb = _sc915.enter_context(self.SB('wsb', [128, 4, NCH * 128], BF16))
        QD0 = _sc915.enter_context(self.PS('QD0', [128, 512], F32))
        QD1 = _sc915.enter_context(self.PS('QD1', [128, 512], F32))
        QD2 = _sc915.enter_context(self.PS('QD2', [128, 512], F32))
        QD3 = _sc915.enter_context(self.PS('QD3', [128, 512], F32))
        QD4 = _sc915.enter_context(self.PS('QD4', [128, 512], F32))
        QD5 = _sc915.enter_context(self.PS('QD5', [128, 512], F32))
        QD6 = _sc915.enter_context(self.PS('QD6', [128, 512], F32))
        QD7 = _sc915.enter_context(self.PS('QD7', [128, 512], F32))
        QDs = [QD0, QD1, QD2, QD3, QD4, QD5, QD6, QD7]
        stPrev = 0
        mmPrev = 0
        evPrev = 0
        for ti, tq in enumerate(tiles):
            P.do("sync", None, wait=[(ST, stPrev), (EV, evPrev)])
            mx = self.load(fxq[:], src3d(tq), s=self.Lp[2])
            P.do("vector", None, wait=[mx, (MM, mmPrev),
                                       (ST, stPrev)])
            P.do("scalar", None, wait=[mx, (MM, mmPrev)])
            P.do("tensor", None, wait=[(EV, evPrev)])
            evLn = self.emit_ln(fxq, fR6, fln, QD6, QD7, QD5,
                                [QD0, QD1, QD2, QD3], fbcQ, scr512=fscr,
                                want_ffn=True)
            bc3 = lambda off: fbcQ[:, off:off + 512][:, None, :] \
                .to_broadcast((128, NCH, 512))
            self.vec(lambda e: e.tensor_tensor(
                out=fR6[:], in0=fln[:], in1=bc3(1024), op=ALU.mult),
                inc=[(EV, 1)])
            evH1 = self.EV.n
            evB = {}
            for ep in range(NCH // 2):
                g = ep % 2
                if ep >= 2:
                    P.do("sync", None, wait=[(MM, evB[("mm", 2 * ep - 3)])])
                else:
                    P.do("sync", None, wait=[(MM, MM.n)])
                wb3 = wsb[:, g * 2:g * 2 + 2, :].rearrange(
                    "p a f -> p (a f)").rearrange("p (c e) -> p c e", e=256)
                mw = self.load(wb3,
                               fm(self.W2D)[:, :, ep * 256:(ep + 1) * 256],
                               s=self.Lp[g])
                for fi in range(2):
                    ec = ep * 2 + fi
                    pq = QDs[ec % 2]
                    for dc in range(NCH):
                        w = ()
                        if dc == 0:
                            w = [mw, (EV, evH1)]
                            if ec >= 2:
                                w.append((EV, evB[ec - 2]))
                        self.mmv(pq[:, 0:512],
                                 wb3[:, dc, fi * 128:(fi + 1) * 128],
                                 fR6[:, dc, :], dc == 0, dc == NCH - 1,
                                 wait=w,
                                 inc=[(MM, 1)] if dc == NCH - 1 else ())
                    evB[("mm", ec)] = self.MM.n
                    self.vec(lambda e, ec=ec, pq=pq: e.scalar_tensor_tensor(
                        out=fh[:, ec, :], in0=pq[:, 0:512],
                        scalar=self.sBW2C[:, ec:ec + 1], in1=fln[:, ec, :],
                        op0=ALU.add, op1=ALU.add),
                        wait=[(MM, evB[("mm", ec)])], inc=[(EV, 1)])
                    evB[ec] = self.EV.n
            evHdone = self.EV.n
            self.act(lambda e: e.activation(
                fR6[:].rearrange("p c t -> p (c t)"),
                fh[:].rearrange("p c t -> p (c t)"), AF.Square),
                wait=[(EV, evHdone)], inc=[(EA, 1)])
            eaH = self.EA.n
            for dc in range(NCH):
                self.mmv(QD6[0:1, 0:512], self.sOnes[:, 0:1], fR6[:, dc, :],
                         dc == 0, dc == NCH - 1,
                         wait=[(EA, eaH)] if dc == 0 else (),
                         inc=[(MM, 1)] if dc == NCH - 1 else ())
            mmH = self.MM.n
            r = lambda o: self.sRow[0:1, o:o + 512]
            self.vec(lambda e: e.tensor_scalar(
                out=r(0), in0=QD6[0:1, 0:512], scalar1=1.0 / D,
                scalar2=None, op0=ALU.mult), wait=[(MM, mmH)],
                inc=[(EV, 1)])
            self.act(lambda e: e.activation(r(512), r(0), AF.Sqrt,
                                            bias=self.sTiny[0:1, 3:4]),
                     wait=[(EV, None)], inc=[(EA, 1)])
            self.vec(lambda e: e.reciprocal(r(1024), r(512)),
                     wait=[(EA, None)], inc=[(EV, 1)])
            self.mmv(QD4[:, 0:512], self.sOnes1f[:], r(1024), True, True,
                     wait=[(EV, None)], inc=[(MM, 1)])
            mmB2 = self.MM.n
            self.vec(lambda e: e.tensor_copy(fbcQ[:, 2048:2560],
                                             QD4[:, 0:512]),
                     wait=[(MM, mmB2)])
            self.vec(lambda e: e.tensor_tensor(
                out=fR5[:], in0=fh[:],
                in1=fbcQ[:, 2048:2560][:, None, :]
                .to_broadcast((128, NCH, 512)),
                op=ALU.mult), inc=[(EV, 1)])
            evH2 = self.EV.n
            evG = {}
            eaG = {}
            mmBlast = self.MM.n
            for ft in range(NFC):
                b = ft % 2
                pg = QDs[2 + (ft % 2) * 2]
                pu = QDs[3 + (ft % 2) * 2]
                if ft < 2:
                    P.do("sync", None, wait=[(MM, mmBlast)])
                else:
                    P.do("sync", None, wait=[(MM, evG[("mm", ft - 2)])])
                mg = self.load(
                    wsb[:, b, :].rearrange("p (c e) -> p c e", e=128),
                    fm(self.WgD)[:, :, ft * 128:(ft + 1) * 128], s=self.Lp[b])
                mu = self.load(
                    wsb[:, b + 2, :].rearrange("p (c e) -> p c e", e=128),
                    fm(self.WuD)[:, :, ft * 128:(ft + 1) * 128],
                    s=self.Lp[2 + b])
                for dc in range(NCH):
                    w = ()
                    if dc == 0:
                        w = [mg, mu, (EV, evH2)]
                        if ft >= 2:
                            w.append((EV, evG[ft - 2]))
                            w.append((EA, eaG[ft - 2]))
                    self.mmv(pg[:, 0:512],
                             wsb[:, b, dc * 128:(dc + 1) * 128],
                             fR5[:, dc, :], dc == 0, dc == NCH - 1, wait=w)
                    self.mmv(pu[:, 0:512],
                             wsb[:, b + 2, dc * 128:(dc + 1) * 128],
                             fR5[:, dc, :], dc == 0, dc == NCH - 1,
                             inc=[(MM, 1)] if dc == NCH - 1 else ())
                evG[("mm", ft)] = self.MM.n
                self.act(lambda e, pg=pg: e.activation(fgl[:], pg[:, 0:512],
                                                       AF.Silu),
                         wait=[(MM, evG[("mm", ft)]), (EV, None)],
                         inc=[(EA, 1)])
                eaG[ft] = self.EA.n
                self.vec(lambda e, ft=ft, pu=pu: e.tensor_tensor(
                    out=fG[:, ft, :], in0=pu[:, 0:512], in1=fgl[:],
                    op=ALU.mult), wait=[(EA, None)], inc=[(EV, 1)])
                evG[ft] = self.EV.n
            evGall = self.EV.n
            evE = 0
            mmGlast = self.MM.n
            FTC = 4
            for eh in range(2):
                mmD = {}
                for fc in range(NFC // FTC):
                    g = fc % 2
                    if fc < 2:
                        P.do("sync", None, wait=[(MM, mmGlast if eh == 0
                                                  else MM.n)])
                    else:
                        P.do("sync", None, wait=[(MM, mmD[fc - 2])])
                    wv3 = wsb[:, g * 2:g * 2 + 2, :].rearrange(
                        "p c (b e) -> p (c b) e", e=1024)
                    md = self.load(
                        wv3,
                        self.WdD[fc * FTC * 128:(fc + 1) * FTC * 128,
                                 eh * 1024:(eh + 1) * 1024]
                        .rearrange("(a p) e -> p a e", p=128),
                        s=self.Lp[g])
                    for fi in range(FTC):
                        ft = fc * FTC + fi
                        for e8 in range(8):
                            w = [md] if (e8 == 0 and fi == 0) else ()
                            if ft == 0 and e8 == 0:
                                w = [md, (EV, evGall), (EA, None)]
                                if eh == 1:
                                    w.append((EV, evE))
                            self.mmv(QDs[e8][:, 0:512],
                                     wv3[:, fi, e8 * 128:(e8 + 1) * 128],
                                     fG[:, ft, :], ft == 0, ft == NFC - 1,
                                     wait=w,
                                     inc=[(MM, 1)] if e8 == 7 else ())
                    mmD[fc] = self.MM.n
                mmE = self.MM.n
                for e8 in range(8):
                    ec = eh * 8 + e8
                    self.vec(lambda e, e8=e8, ec=ec: e.tensor_tensor(
                        out=fh[:, ec, :], in0=QDs[e8][:, 0:512],
                        in1=fh[:, ec, :], op=ALU.add),
                        wait=[(MM, mmE)] if e8 == 0 else (),
                        inc=[(EV, 1)] if e8 == 7 else ())
                evE = self.EV.n
            m01b3 = fbcQ[:, 1536:2048][:, None, :].to_broadcast(
                (128, NCH, 512))
            self.vec(lambda e: e.tensor_tensor(out=fR6[:], in0=fh[:],
                                               in1=fln[:],
                                               op=ALU.subtract),
                     wait=[(EV, evE)])
            self.vec(lambda e: e.tensor_tensor(out=fR5[:], in0=fR6[:],
                                               in1=m01b3, op=ALU.mult))
            self.vec(lambda e: e.tensor_tensor(out=fh[:], in0=fR5[:],
                                               in1=fln[:], op=ALU.add))
            self.vec(lambda e: e.tensor_tensor(out=fxq[:], in0=fh[:],
                                               in1=fxq[:], op=ALU.add),
                     inc=[(EV, 1)])
            evFin = self.EV.n
            P.do("gpsimd", None, wait=[(EV, evFin)])
            self.store(dst3d(tq), fxq[:])
            stPrev = self.ST.n
            mmPrev = self.MM.n
            evPrev = evFin
        _sc915.close()
        self.bar()

    def s2(self):
        src = lambda tq: fm(self.xr)[:, :, tq * 512:(tq + 1) * 512]
        dst = lambda tq: fm(self.xr)[:, :, tq * 512:(tq + 1) * 512]
        self.ffn_quarter(src, dst, [0, 1, 2, 3, 4], True)

    def s4(self):
        src = lambda tq: fm(self.xr)[:, :, HALO + tq * 512:HALO + (tq + 1) * 512]
        dst = lambda tq: fm(self.out)[:, :, tq * 512:(tq + 1) * 512]
        self.ffn_quarter(src, dst, [0, 1, 2, 3], False)

    def s3(self):
        P, LD, ST, MM, EV, EA = self.P, self.LD, self.ST, self.MM, self.EV, self.EA
        nc, I = self.nc, self.inp
        _sc1121 = ExitStack()
        xt3 = _sc1121.enter_context(self.SB('xt3', [128, NCH, 512], F32))
        lnb3 = _sc1121.enter_context(self.SB('lnb3', [128, NCH, 512], BF16))
        xsq3 = _sc1121.enter_context(self.SB('xsq3', [128, NCH, 512], BF16))
        scr3 = _sc1121.enter_context(self.SB('scr3', [128, 512], F32))
        bcQ3 = _sc1121.enter_context(self.SB('bcQ3', [128, 1024], F32))
        pA3 = _sc1121.enter_context(self.PS('pA3', [128, 512], F32))
        pB3 = _sc1121.enter_context(self.PS('pB3', [128, 1024], F32))
        mmPrev = 0
        stPrev = 0
        evPrev = 0
        for t in range(5):
            P.do("sync", None, wait=[(EV, evPrev), (ST, stPrev),
                                     (EA, None)])
            mx = self.load(xt3[:], fm(self.xr)[:, :, t * 512:(t + 1) * 512], s=self.Lp[2])
            P.do("vector", None, wait=[mx, (MM, mmPrev)])
            P.do("scalar", None, wait=[mx, (MM, mmPrev)])
            P.do("tensor", None, wait=[(EV, evPrev)])
            evPrev = self.emit_ln(xt3, xsq3, lnb3, pA3, pB3, None,
                                  [pB3[:, 0:512], pB3[:, 512:1024]], bcQ3,
                                  scr512=scr3)
            self.store(fm(self.lnbuf)[:, :, t * 512:(t + 1) * 512], lnb3[:],
                       wait=[(EV, evPrev)])
            stPrev = self.ST.n
            mmPrev = self.MM.n
        _sc1121.close()
        self.bar()

        _sc1146 = ExitStack()
        QT = _sc1146.enter_context(self.SB('QT', [128, NCH, OWN], BF16))
        lnq3 = _sc1146.enter_context(self.SB('lnq3', [128, NCH, 512], BF16))
        _sc1151 = ExitStack()
        WqS = _sc1151.enter_context(self.SB('WqS', [128, NCH, D], BF16))
        wld3 = _sc1151.enter_context(self.SB('wld3', [128, 2, D], F32))
        pQ3a = _sc1151.enter_context(self.PS('pQ3a', [128, 512], F32))
        pQ3b = _sc1151.enter_context(self.PS('pQ3b', [128, 512], F32))
        ev_c = {}
        for cc in range(NCH):
            b = cc % 2
            if cc >= 2:
                P.do("sync", None, wait=[(EV, ev_c[cc - 2])])
            m = self.load(wld3[:, b, :],
                          I["Wq_b"][cc * 128:(cc + 1) * 128, :],
                          s=self.Lp[b])
            self.vec(lambda e, b=b, cc=cc: e.tensor_copy(
                WqS[:, cc, :], wld3[:, b, :]), wait=[m],
                inc=[(EV, 1)])
            ev_c[cc] = EV.n
        evW = EV.n
        for tt in range(4):
            P.do("sync", None, wait=[(EV, None)])
            mln = self.load(lnq3[:],
                            fm(self.lnbuf)[:, :, HALO + tt * 512:
                                           HALO + (tt + 1) * 512], s=self.Lp[2])
            evQ = {}
            for qc in range(NCH):
                pq = pQ3a if qc % 2 == 0 else pQ3b
                for dc in range(NCH):
                    w = ()
                    if dc == 0:
                        w = [mln, (EV, evW)]
                        if qc >= 2:
                            w.append((EV, evQ[qc - 2]))
                    self.mmv(pq[:, 0:512],
                             WqS[:, dc, qc * 128:(qc + 1) * 128],
                             lnq3[:, dc, :], dc == 0, dc == NCH - 1,
                             wait=w,
                             inc=[(MM, 1)] if dc == NCH - 1 else ())
                mmQ = self.MM.n
                self.vec(lambda e, qc=qc, tt=tt, pq=pq: e.tensor_scalar(
                    out=QT[:, qc, tt * 512:(tt + 1) * 512],
                    in0=pq[:, 0:512], scalar1=1.0 / HD, scalar2=None,
                    op0=ALU.mult), wait=[(MM, mmQ)], inc=[(EV, 1)])
                evQ[qc] = self.EV.n
        _sc1151.close()
        self.bar()

        KT = _sc1146.enter_context(self.SB('KT', [128, HKV, EXT], BF16))
        VT = _sc1146.enter_context(self.SB('VT', [128, EXT // 128, 512], BF16))
        _sc1195 = ExitStack()
        WkS = _sc1195.enter_context(self.SB('WkS', [128, NCH, 512], BF16))
        WvS = _sc1195.enter_context(self.SB('WvS', [128, NCH, 512], BF16))
        wld4 = _sc1195.enter_context(self.SB('wld4', [128, 2, 512], F32))
        pK3a = _sc1195.enter_context(self.PS('pK3a', [128, 512], F32))
        pK3b = _sc1195.enter_context(self.PS('pK3b', [128, 512], F32))
        ev_c2 = {}
        for i in range(2 * NCH):
            mat, dstb = (I["Wk_b"], WkS) if i % 2 == 0 else \
                        (I["Wv_b"], WvS)
            cc = i // 2
            b = i % 2
            if i >= 2:
                P.do("sync", None, wait=[(EV, ev_c2[i - 2])])
            m = self.load(wld4[:, b, :],
                          mat[cc * 128:(cc + 1) * 128, :], s=self.Lp[b])
            self.vec(lambda e, b=b, cc=cc, dstb=dstb: e.tensor_copy(
                dstb[:, cc, :], wld4[:, b, :]), wait=[m],
                inc=[(EV, 1)])
            ev_c2[i] = EV.n
        evW2 = EV.n
        for tt in range(5):
            P.do("sync", None, wait=[(EV, None), (EA, None)])
            mln = self.load(lnq3[:],
                            fm(self.lnbuf)[:, :, tt * 512:(tt + 1) * 512], s=self.Lp[2])
            prevK = {}
            for gc in range(HKV):
                pq = pK3a if gc % 2 == 0 else pK3b
                for dc in range(NCH):
                    w = ()
                    if dc == 0:
                        w = [mln, (EV, evW2), (EA, None)]
                    self.mmv(pq[:, 0:512],
                             WkS[:, dc, gc * 128:(gc + 1) * 128],
                             lnq3[:, dc, :], dc == 0, dc == NCH - 1,
                             wait=w,
                             inc=[(MM, 1)] if dc == NCH - 1 else ())
                mmK = self.MM.n
                self.act(lambda e, gc=gc, tt=tt, pq=pq: e.activation(
                    KT[:, gc, tt * 512:(tt + 1) * 512], pq[:, 0:512],
                    AF.Copy), wait=[(MM, mmK)], inc=[(EA, 1)])
            for sub in range(4):
                pq = pK3a if sub % 2 == 0 else pK3b
                for dc in range(NCH):
                    w = ()
                    if dc == 0:
                        w = [(EA, None), (EV, None)]
                    self.mmv(pq[:, 0:512],
                             lnq3[:, dc, sub * 128:(sub + 1) * 128],
                             WvS[:, dc, :], dc == 0, dc == NCH - 1,
                             wait=w,
                             inc=[(MM, 1)] if dc == NCH - 1 else ())
                mmV = self.MM.n
                self.vec(lambda e, sub=sub, tt=tt, pq=pq: e.tensor_copy(
                    VT[:, tt * 4 + sub, :], pq[:, 0:512]),
                    wait=[(MM, mmV)], inc=[(EV, 1)])
        _sc1195.close()
        self.bar()

        _sc1253 = ExitStack()
        Ssb = _sc1253.enter_context(self.SB('Ssb', [128, 640], F32))
        Pn = _sc1253.enter_context(self.SB('Pn', [128, 640], BF16))
        PTs = _sc1253.enter_context(self.SB('PTs', [128, 5, 128], BF16))
        nmx = _sc1253.enter_context(self.SB('nmx', [128, 2], F32))
        aO = _sc1253.enter_context(self.SB('aO', [128, NCH, 512], BF16))
        xq3b = _sc1253.enter_context(self.SB('xq3b', [128, NCH, 512], F32))
        wob3 = _sc1253.enter_context(self.SB('wob3', [128, 4, 512], BF16))
        pScA = _sc1253.enter_context(self.PS('pScA', [128, 640], F32))
        pScB = _sc1253.enter_context(self.PS('pScB', [128, 640], F32))
        pPT = _sc1253.enter_context(self.PS('pPT', [128, 640], BF16))
        pO = _sc1253.enter_context(self.PS('pO', [128, 512], F32))
        pX = _sc1253.enter_context(self.PS('pX', [128, 512], F32))
        pY = _sc1253.enter_context(self.PS('pY', [128, 512], F32))
        sMaskR = _sc1253.enter_context(self.SB("sMaskR", [128, 640], BF16))
        sMaskF = _sc1253.enter_context(self.SB("sMaskF", [128, 4, 640], BF16))
        self.load(sMaskR[:], I["maskR"])
        self.load(sMaskF[:], I["maskF"].rearrange("(j p) k -> p j k", p=128))
        P.do("vector", None, wait=[(self.LD, None)])
        stPrev = 0
        evMA = {}
        uix = 0
        for j3 in range(4):
            P.do("sync", None, wait=[(EV, None), (ST, stPrev)])
            mxj = self.load(xq3b[:],
                            fm(self.xr)[:, :, HALO + j3 * 512:
                                        HALO + (j3 + 1) * 512],
                            s=self.Lp[3])
            evAO = 0
            for hh in range(HQ):
                g = hh // 4
                for qt in range(4):
                    koff = j3 * 512 + qt * 128
                    pS = pScA if uix % 2 == 0 else pScB
                    wS = [(EV, evMA.get(uix - 2, 0))]
                    self.mmv(pS[:, 0:512],
                             QT[:, hh, j3 * 512 + qt * 128:
                                j3 * 512 + (qt + 1) * 128],
                             KT[:, g, koff:koff + 512], True, True, wait=wS)
                    self.mmv(pS[:, 512:640],
                             QT[:, hh, j3 * 512 + qt * 128:
                                j3 * 512 + (qt + 1) * 128],
                             KT[:, g, koff + 512:koff + 640],
                             True, True, inc=[(MM, 1)])
                    mmS = self.MM.n
                    msk = (sMaskF[:, qt, :] if j3 == 0 else sMaskR[:])
                    self.vec(lambda e, msk=msk, pS=pS: e.tensor_tensor(
                        out=Ssb[:], in0=pS[:], in1=msk, op=ALU.add),
                        wait=[(MM, mmS), (EA, None)])
                    evMA[uix] = self.EV.n
                    self.vec(lambda e: e.reduce_max(
                        nmx[:, 0:1], Ssb[:], axis=AXX, negate=True),
                        inc=[(EV, 1)])
                    evM = self.EV.n
                    self.act(lambda e: e.activation(
                        Pn[:], Ssb[:], AF.Exp, bias=nmx[:, 0:1],
                        accum_out=nmx[:, 1:2]),
                        wait=[(EV, evM), (MM, None)],
                        inc=[(EA, 1)])
                    eaE = self.EA.n
                    self.vec(lambda e: e.reciprocal(
                        nmx[:, 1:2], nmx[:, 1:2]), wait=[(EA, eaE)])
                    self.vec(lambda e: e.tensor_scalar(
                        out=Pn[:], in0=Pn[:], scalar1=nmx[:, 1:2],
                        scalar2=None, op0=ALU.mult), inc=[(EV, 1)])
                    evP = self.EV.n
                    for s5 in range(5):
                        self.P.do("tensor", lambda e, s5=s5: e.transpose(
                            pPT[:, s5 * 128:(s5 + 1) * 128],
                            Pn[:, s5 * 128:(s5 + 1) * 128],
                            self.sIDN[:]),
                            wait=[(EV, evP)] if s5 == 0 else (),
                            inc=[(MM, 1)] if s5 == 4 else ())
                    mmT = self.MM.n
                    self.vec(lambda e: e.tensor_copy(
                        PTs[:].rearrange("p a b -> p (a b)"), pPT[:]),
                        wait=[(MM, mmT)], inc=[(EV, 1)])
                    evPT = self.EV.n
                    for s5 in range(5):
                        rsub = j3 * 4 + qt + s5
                        self.mmv(pO[:, qt * 128:(qt + 1) * 128],
                                 VT[:, rsub, g * 128:(g + 1) * 128],
                                 PTs[:, s5, :],
                                 s5 == 0, s5 == 4,
                                 wait=[(EV, evPT), (EA, None)]
                                 if s5 == 0 else (),
                                 inc=[(MM, 1)] if s5 == 4 else ())
                    uix += 1
                mmPV = self.MM.n
                self.act(lambda e, hh=hh: e.activation(
                    aO[:, hh, :], pO[:, 0:512], AF.Copy),
                    wait=[(MM, mmPV)], inc=[(EA, 1)])
                evAO = self.EA.n
            pb3 = [pO, pX, pY]
            for sw in range(6):
                e0 = sw * 3
                ne = min(3, NCH - e0)
                mmWO = {}
                for cp in range(NCH // 2):
                    g = cp % 2
                    if cp >= 2:
                        P.do("sync", None, wait=[(MM, mmWO[cp - 2])])
                    else:
                        P.do("sync", None, wait=[(MM, MM.n)])
                    wv3 = wob3[:, g * 2:g * 2 + 2, 0:ne * 128]
                    mwo = self.load(
                        wv3,
                        self.WobD[cp * 256:(cp + 1) * 256,
                                  e0 * 128:(e0 + ne) * 128]
                        .rearrange("(a p) e -> p a e", p=128),
                        s=self.Lp[g])
                    for fi in range(2):
                        cc = cp * 2 + fi
                        for e4 in range(ne):
                            w = [mwo]
                            if cc == 0:
                                w = [mwo, (EA, evAO), (EV, None)]
                            self.mmv(pb3[e4][:, 0:512],
                                     wv3[:, fi, e4 * 128:(e4 + 1) * 128],
                                     aO[:, cc, :], cc == 0, cc == NCH - 1,
                                     wait=w if (e4 == 0 or cc == 0) else (),
                                     inc=[(MM, 1)] if e4 == ne - 1 else ())
                    mmWO[cp] = self.MM.n
                mmO = self.MM.n
                for e4 in range(ne):
                    ec = e0 + e4
                    self.vec(lambda e, e4=e4, ec=ec: e.tensor_tensor(
                        out=xq3b[:, ec, :], in0=pb3[e4][:, 0:512],
                        in1=xq3b[:, ec, :], op=ALU.add),
                        wait=[(MM, mmO), mxj] if e4 == 0 else (),
                        inc=[(EV, 1)] if e4 == ne - 1 else ())
            self.store(fm(self.xr)[:, :, HALO + j3 * 512:
                                   HALO + (j3 + 1) * 512], xq3b[:],
                       wait=[(EV, EV.n)])
            stPrev = self.ST.n
        P.do("sync", None, wait=[(ST, None)])
        _sc1253.close()
        _sc1146.close()
        self.bar()




def _band_mask():
    qi = np.arange(128)[:, None]
    ki = np.arange(640)[None, :]
    ok = (ki >= qi + 1) & (ki <= qi + WIN)
    return np.where(ok, 0.0, NEG).astype(np.float32)


WIN = 512


def _first_masks(h):
    base = _band_mask()
    out = np.zeros((4, 128, 640), np.float32)
    for qt in range(4):
        m = base.copy()
        if h == 0:
            ki = np.arange(640)[None, :]
            m = np.where(qt * 128 + ki >= 512, m, NEG)
        out[qt] = m
    return out.reshape(512, 640).astype(np.float32)


def make_inputs(core, kw):
    b, h = core // 2, core % 2
    x = np.asarray(kw["L0_attn_a_hidden_states"], np.float32)
    own = x[b, h * OWN:(h + 1) * OWN]
    other = x[b, (1 - h) * OWN:(2 - h) * OWN]
    halo = (x[b, OWN - HALO:OWN] if h == 1
            else np.zeros((HALO, D), np.float32))
    ext9 = np.concatenate([halo, own, other], axis=0)
    f32 = lambda a: np.ascontiguousarray(np.asarray(a, np.float32))
    d = {
        "x9": f32(ext9.T),
        "wq_a": f32(kw["L0_attn_a_wq"]),
        "wk_a": f32(kw["L0_attn_a_wk"]),
        "wvTa": f32(np.asarray(kw["L0_attn_a_wv"], np.float32).T),
        "woTa": f32(np.asarray(kw["L0_attn_a_wo"], np.float32).T),
        "wmq": f32(kw["L0_attn_a_wmap_q"]),
        "wmk": f32(kw["L0_attn_a_wmap_k"]),
        "bq_a": f32(kw["L0_attn_a_bq"]),
        "bk_a": f32(kw["L0_attn_a_bk"]),
        "bv_a": f32(kw["L0_attn_a_bv"]),
        "bo_a": f32(kw["L0_attn_a_bo"]),
        "Wq_b": f32(kw["L0_attn_b_Wq"]),
        "Wk_b": f32(kw["L0_attn_b_Wk"]),
        "Wv_b": f32(kw["L0_attn_b_Wv"]),
        "Wo_b": f32(kw["L0_attn_b_Wo"]),
        "wvTf": f32(np.asarray(kw["L0_ffn_wv"], np.float32).T),
        "wo_f": f32(kw["L0_ffn_wo"]),
        "w_r": f32(kw["L0_ffn_w_router"]),
        "b_r": f32(np.asarray(kw["L0_ffn_b_router"], np.float32).reshape(1)),
        "n1w": f32(kw["L0_ffn_norm1_w"]),
        "n2w": f32(kw["L0_ffn_norm2_w"]),
        "bv_f": f32(kw["L0_ffn_bv"]),
        "Wg": f32(kw["L0_ffn_w_gate"]),
        "Wu": f32(kw["L0_ffn_w_up"]),
        "Wd": f32(kw["L0_ffn_w_down"]),
        "maskR": _band_mask().astype(ml_dtypes.bfloat16),
        "maskF": _first_masks(h).astype(ml_dtypes.bfloat16),
        "IDN": np.eye(128, dtype=np.float32).astype(ml_dtypes.bfloat16),
        "IDNF": np.eye(128, dtype=np.float32),
        "onesb": np.stack([np.ones(128),
                           (np.arange(128) < 64).astype(np.float32),
                           (np.arange(128) >= 64).astype(np.float32),
                           np.zeros(128)], axis=1).astype(ml_dtypes.bfloat16),
        "ones1f": np.ones((1, 128), np.float32),
        "epsv": np.array([[EPS_LN, EPS_RMS]], np.float32),
    }
    return d


def build(upto=None, dev_taps=()):
    B = Builder()
    phases = [("p0a", B.p0a), ("p0b", B.p0b), ("p0w", B.p0w),
              ("s1a", B.s1a), ("s1b", B.s1b), ("s2", B.s2),
              ("s3", B.s3), ("s4", B.s4)]
    for name, fn in phases:
        fn()
        if name == upto:
            break
    for tname in dev_taps:
        ap, dt, shape = B.taps[tname]
        o = B.nc.dram_tensor("tap_" + tname, shape, dt,
                             kind="ExternalOutput").ap()
        B.P.do("gpsimd", None, wait=[(B.ST, B.ST.n)])
        B.P.do("gpsimd", lambda e, o=o, ap=ap: e.dma_start(out=o, in_=ap),
               inc=[(B.ST, 16)])
    B.P.do("gpsimd", None, wait=[(B.ST, None)])
    B.P.do("sync", None, wait=[(B.ST, None), (B.LD, None)])
    with B.nc.Block() as block:
        B.P.replay(block)
    B.kv_ctx.close()
    B.ctx.close()
    return B.nc


def kernel(**inputs):
    nc = build()
    in_maps = [make_inputs(c, inputs) for c in range(8)]
    res = run_bass_kernel_spmd(nc, in_maps, core_ids=list(range(8)))
    x = np.zeros((4, FULL, D), np.float32)
    for c in range(8):
        b, h = c // 2, c % 2
        o = np.asarray(res.results[c]["out"])
        x[b, h * OWN:(h + 1) * OWN, :] = o.T
    return x
